# version 16
# baseline (speedup 1.0000x reference)
import contextlib
import numpy as np
import ml_dtypes
import concourse.bass as bass
import concourse.mybir as mybir
from concourse.bass_utils import run_bass_kernel_spmd

F32 = mybir.dt.float32
BF16 = mybir.dt.bfloat16
U8 = mybir.dt.uint8
AF = mybir.ActivationFunctionType
ALU = mybir.AluOpType
AX = mybir.AxisListType
bf16 = ml_dtypes.bfloat16

MAGIC = 1.5 * 2.0 ** 23
NCORES = 8
B, S, D, H, HD, I3 = 2, 2048, 1024, 16, 64, 3072
NT, DC, IC = S // 128, D // 128, I3 // 128
SB, NSB, TG = 512, S // 512, B * S // 512
VROW = 66
EPS = 1e-5
QS = 32767.0
ENGS = ("sync", "gpsimd", "tensor", "vector", "scalar")
NDMASEM = 8
DMA_QUEUES = ("sync", "gpsimd", "scalar_dma")
DMA_SEMS = tuple(f"{q}{i}" for q in DMA_QUEUES for i in range(NDMASEM))


class Plan:
    def __init__(self):
        self.steps = {e: [] for e in ENGS}
        self.cnt = {}

    def op(self, eng, emit, deps=(), sig=False, dma=False, cc=False):
        waits = {}
        for d in deps:
            if d is None:
                continue
            sn, v = d
            if waits.get(sn, 0) < v:
                waits[sn] = v
        sem, inc, tok = None, 1, None
        if cc:
            sem = "cc"
        elif dma:
            q = "scalar_dma" if eng == "scalar" else eng
            i = self.cnt.get("_q_" + q, 0)
            self.cnt["_q_" + q] = i + 1
            sem, inc = f"{q}{i % NDMASEM}", 16
            prev = self.cnt.get(sem, 0)
            if prev and waits.get(sem, 0) < prev:
                waits[sem] = prev
        elif sig:
            sem = eng
        if sem is not None:
            self.cnt[sem] = self.cnt.get(sem, 0) + inc
            tok = (sem, self.cnt[sem])
        self.steps[eng].append((waits, emit, sem, inc))
        return tok

    def replay(self, eng_name, eng, sems):
        done = {}
        for waits, emit, sem, inc in self.steps[eng_name]:
            for sn in sorted(waits):
                v = waits[sn]
                if (sn == eng_name == "tensor"):
                    continue
                if done.get(sn, 0) >= v:
                    continue
                eng.wait_ge(sems[sn], v)
                done[sn] = v
            ins = emit(eng)
            if sem is not None:
                ins.then_inc(sems[sem], inc)


class Arena:
    def __init__(self, handle, nbytes):
        self.h, self.nbytes, self.off, self.marks = handle, nbytes, 0, []

    def push(self):
        self.marks.append(self.off)

    def pop(self):
        self.off = self.marks.pop()

    def alloc(self, cols, dtype, rows=128):
        nb = cols * mybir.dt.size(dtype)
        off = self.off
        self.off += (nb + 63) & ~63
        assert self.off <= self.nbytes, f"SBUF overflow {self.off}>{self.nbytes}"
        return self.h.ap()[0:rows, off:off + nb].bitcast(dtype)


def _ropetables():
    inv = 1.0 / (10000.0 ** (np.arange(0, HD, 2, dtype=np.float64) / HD))
    fr = np.outer(np.arange(S, dtype=np.float64), inv)
    return (np.tile(np.cos(fr), (1, H)).astype(bf16),
            np.tile(np.sin(fr), (1, H)).astype(bf16))


def _img(w, pchunks):
    r, x = w.shape
    assert r == pchunks * 128
    return np.ascontiguousarray(
        w.reshape(pchunks, 128, x).transpose(1, 0, 2).reshape(128, pchunks * x))


def _z2img(wt, pchunks, fchunks):
    z = wt.reshape(pchunks, 128, fchunks, 128).transpose(0, 3, 2, 1)
    return np.ascontiguousarray(
        z.reshape(pchunks, 128, fchunks * 128).transpose(1, 0, 2).reshape(
            128, pchunks * fchunks * 128))


def _stage(inputs):
    perm = np.concatenate([np.arange(0, HD, 2), np.arange(1, HD, 2)])
    permg = (np.arange(H)[:, None] * HD + perm[None, :]).reshape(-1)
    wq = np.asarray(inputs["L0_attn_Wq"])[permg, :]
    wk = np.asarray(inputs["L0_attn_Wk"])[permg, :]
    wv = np.asarray(inputs["L0_attn_Wv"])
    wo = np.asarray(inputs["L0_attn_Wo"])
    wupT = np.ascontiguousarray(np.asarray(inputs["L0_ffn_w_up"], np.float32).T)
    wgT = np.ascontiguousarray(np.asarray(inputs["L0_ffn_w_gate"], np.float32).T)
    wdT = np.ascontiguousarray(np.asarray(inputs["L0_ffn_w_down"], np.float32).T)
    rc, rs = _ropetables()
    shared = {
        "wqT": _img(np.ascontiguousarray(wq.T), DC).astype(bf16),
        "wkT": _img(np.ascontiguousarray(wk.T), DC).astype(bf16),
        "wvT": _img(np.ascontiguousarray(wv.T), DC).astype(bf16),
        "woT": _img(np.ascontiguousarray(wo.T), DC).astype(bf16),
        "bq": np.asarray(inputs["L0_attn_bq"])[permg].reshape(1, D).astype(bf16),
        "bk": np.asarray(inputs["L0_attn_bk"])[permg].reshape(1, D).astype(bf16),
        "bv": np.asarray(inputs["L0_attn_bv"]).reshape(1, D).astype(bf16),
        "bo_col": np.ascontiguousarray(
            np.asarray(inputs["L0_attn_bo"], np.float32).reshape(DC, 128).T),
        "wupZ": _z2img(wupT, DC, IC),
        "wgZ": _z2img(wgT, DC, IC),
        "wdZ": _z2img(wdT, IC, DC),
        "ropec": rc, "ropes": rs,
        "ones1": np.ones((1, 128), dtype=bf16),
        "identf": np.eye(128, dtype=np.float32),
        "blockones": np.ascontiguousarray(np.stack(
            [np.r_[np.ones(64), np.zeros(64)],
             np.r_[np.zeros(64), np.ones(64)]], axis=1)).astype(bf16),
    }
    x = np.asarray(inputs["L0_attn_x"], dtype=np.float32)
    return [dict(shared, x=np.ascontiguousarray(x[2 * c:2 * c + 2]))
            for c in range(NCORES)]


def build_program(tap=None, phases=("dq", "attn0", "attn1", "ffn")):
    nc = bass.Bass("TRN2", target_bir_lowering=False)

    xh = nc.dram_tensor("x", [B, S, D], F32, kind="ExternalInput").ap()
    w_h = {k: nc.dram_tensor(k, [128, DC * D], BF16, kind="ExternalInput").ap()
           for k in ("wqT", "wkT", "wvT", "woT")}
    b_h = {k: nc.dram_tensor(k, [1, D], BF16, kind="ExternalInput").ap()
           for k in ("bq", "bk", "bv")}
    boh = nc.dram_tensor("bo_col", [128, DC], F32, kind="ExternalInput").ap()
    wZh = {"up": nc.dram_tensor("wupZ", [128, DC * I3], F32, kind="ExternalInput").ap(),
           "g": nc.dram_tensor("wgZ", [128, DC * I3], F32, kind="ExternalInput").ap(),
           "d": nc.dram_tensor("wdZ", [128, IC * D], F32, kind="ExternalInput").ap()}
    ropech = nc.dram_tensor("ropec", [S, 512], BF16, kind="ExternalInput").ap()
    ropesh = nc.dram_tensor("ropes", [S, 512], BF16, kind="ExternalInput").ap()
    ones1h = nc.dram_tensor("ones1", [1, 128], BF16, kind="ExternalInput").ap()
    idfh = nc.dram_tensor("identf", [128, 128], F32, kind="ExternalInput").ap()
    blkoh = nc.dram_tensor("blockones", [128, 2], BF16, kind="ExternalInput").ap()
    outh = nc.dram_tensor("out", [B, S, D], F32, kind="ExternalOutput").ap()

    x1_scr = nc.dram_tensor("x1_scr", [B * NT, 128, D], F32).ap()
    xq_scr = nc.dram_tensor("xq_scr", [128, DC * B * S], BF16).ap()
    spill = {k: nc.dram_tensor(k + "_scr", [TG, IC, 128, 512], BF16).ap()
             for k in ("up", "gate", "g2", "o")}
    wdq_h = {"up": nc.dram_tensor("wup_dq", [128, DC * I3], BF16).ap(),
             "g": nc.dram_tensor("wg_dq", [128, DC * I3], BF16).ap(),
             "d": nc.dram_tensor("wd_dq", [128, IC * D], BF16).ap()}
    lcol_d = nc.dram_tensor("lcol_d", [2, (H // 2) * SB], BF16).ap()
    lrec_d = nc.dram_tensor("lrec_d", [H, SB], BF16).ap()
    sc_d = nc.dram_tensor("sc_d", [1, 32], F32).ap()
    col_d = nc.dram_tensor("col_d", [1, 128], F32).ap()
    ncol_d = nc.dram_tensor("ncol_d", [2, 128], F32).ap()
    cc_io = [(nc.dram_tensor(f"cc_in{i}", [1, 8], F32).ap(),
              nc.dram_tensor(f"cc_out{i}", [1, 8], F32).ap()) for i in range(4)]

    dbg = None
    if tap is not None:
        shp = {"ffn4": ([4, TG, IC, 128, 512], BF16),
               "x1": ([B * NT, 128, D], F32), "xq": ([128, DC * B * S], BF16),
               "up": ([TG, IC, 128, 512], BF16), "gate": ([TG, IC, 128, 512], BF16),
               "g2": ([TG, IC, 128, 512], BF16), "o": ([TG, IC, 128, 512], BF16),
               "wupdq": ([128, DC * I3], BF16), "wgdq": ([128, DC * I3], BF16),
               "wddq": ([128, IC * D], BF16)}[tap]
        dbg = nc.dram_tensor("dbg", shp[0], shp[1], kind="ExternalOutput").ap()

    P = Plan()
    es = contextlib.ExitStack()
    with es:
        psh = es.enter_context(nc.psum_tensor("ps", [128, 4096], F32))
        psap = psh.ap()

        def bank(i, n=1, rows=128):
            return psap[0:rows, i * 512:(i + n) * 512]

        ARENA_B = 200 * 1024
        sbh = es.enter_context(nc.sbuf_tensor("arena", [128, ARENA_B], U8))
        A = Arena(sbh, ARENA_B)
        sems = {n: es.enter_context(nc.semaphore("s_" + n))
                for n in ("tensor", "vector", "scalar", "gpsimd", "cc") + DMA_SEMS}

        ones1 = A.alloc(128, BF16, rows=1)
        blko = A.alloc(2, BF16)
        idf = A.alloc(128, F32)
        bsb = {k: A.alloc(D, BF16, rows=1) for k in ("bq", "bk", "bv")}
        bo_sb = A.alloc(DC, F32)
        scal = A.alloc(16, F32)
        SCL = dict(negm0=0, sx=1, isx=2, sup=3, isup=4, sg=5, isg=6,
                   sg2=7, isg2=8, so=9, iso=10, gtmp=11)

        def scol(n):
            return scal[:, SCL[n]:SCL[n] + 1]

        stat = A.alloc(8, F32)
        smallf = A.alloc(256, F32)
        amax = A.alloc(16, F32)
        lnj = A.alloc(2048, BF16)
        xf32 = [A.alloc(D, F32) for _ in range(2)]
        x1t = A.alloc(D, F32)
        lntmp = [A.alloc(D, BF16) for _ in range(2)]

        t0 = []
        for src, dst in ((ones1h, ones1), (blkoh, blko), (idfh, idf),
                         (boh, bo_sb), (b_h["bq"], bsb["bq"]),
                         (b_h["bk"], bsb["bk"]), (b_h["bv"], bsb["bv"])):
            t0.append(P.op("sync", lambda e, s=src, d=dst:
                           e.dma_start(out=d, in_=s[:, :]), dma=True))
        tok_const = t0[-1]

        WAR0 = {}

        def ts(eng, out, in0, s1, s2, op0, op1=None, accum=None, deps=()):
            kw = {}
            if accum is not None:
                kw["accum_out"] = accum
            if op1 is not None:
                return P.op(eng, lambda e, o=out, i=in0, a=s1, b=s2, kw=kw:
                            e.tensor_scalar(o, i, a, b, op0, op1, **kw),
                            deps=deps, sig=True)
            return P.op(eng, lambda e, o=out, i=in0, a=s1, kw=kw:
                        e.tensor_scalar(o, i, a, None, op0, **kw),
                        deps=deps, sig=True)

        def tt(eng, out, in0, in1, op, deps=()):
            return P.op(eng, lambda e, o=out, a=in0, b=in1:
                        e.tensor_tensor(o, a, b, op), deps=deps, sig=True)

        def vcopy(out, in0, deps=()):
            return P.op("vector", lambda e, o=out, i=in0: e.tensor_copy(o, i),
                        deps=deps, sig=True)

        def scopy(out, in0, deps=()):
            return P.op("scalar", lambda e, o=out, i=in0: e.copy(o, i),
                        deps=deps, sig=True)

        def ln_stats(xtile, slot, deps):
            c = slot * 4
            su, sq = stat[:, c:c + 1], stat[:, c + 1:c + 2]
            mn, rs = stat[:, c + 2:c + 3], stat[:, c + 3:c + 4]
            w = smallf[:, 16 + 4 * slot:16 + 4 * slot + 3]
            nfd = xtile.shape[-1]
            t1 = P.op("vector", lambda e, x=xtile, o=su: e.reduce_sum(o, x, axis=AX.X),
                      deps=deps, sig=True)
            t2 = P.op("scalar", lambda e, x=xtile, o=sq, n=nfd: e.activation(
                lnj[:, 0:n], x, AF.Square, accum_out=o), deps=deps, sig=True)
            t3 = ts("vector", mn, su, 1.0 / D, None, ALU.mult, deps=[t1])
            t4 = tt("vector", w[:, 0:1], mn, mn, ALU.mult, deps=[t3])
            t5 = ts("vector", w[:, 1:2], sq, 1.0 / D, None, ALU.mult, deps=[t2])
            t6 = tt("vector", w[:, 1:2], w[:, 1:2], w[:, 0:1], ALU.subtract,
                    deps=[t4, t5])
            t7 = ts("vector", w[:, 1:2], w[:, 1:2], EPS, None, ALU.add, deps=[t6])
            t8 = P.op("scalar", lambda e, i=w[:, 1:2], o=w[:, 2:3]: e.activation(
                o, i, AF.Sqrt), deps=[t7], sig=True)
            t9 = P.op("vector", lambda e, i=w[:, 2:3], o=rs: e.reciprocal(o, i),
                      deps=[t8], sig=True)
            return mn, rs, t9, t2

        def collapse2(col_ap, deps, slot):
            ta = P.op("gpsimd", lambda e, c=col_ap: e.dma_start(
                out=col_d[0:1, :], in_=c), deps=list(deps) + [WAR0.get("colD")],
                dma=True)
            tb = P.op("sync", lambda e: e.dma_start(
                out=smallf[0:1, 64:192], in_=col_d[0:1, :]), deps=[ta], dma=True)
            WAR0["colD"] = tb
            tc = P.op("vector", lambda e: e.reduce_max(
                smallf[0:1, 40:41], smallf[0:1, 64:192], axis=AX.X),
                deps=[tb], sig=True)
            td = P.op("gpsimd", lambda e, s=slot: e.dma_start(
                out=sc_d[0:1, s:s + 1], in_=smallf[0:1, 40:41]),
                deps=[tc], dma=True)
            return td, tb

        def bcast_col(dram_ap, col, deps):
            return P.op("sync", lambda e, d=dram_ap, c=col: e.dma_start(
                out=c, in_=d.broadcast_to([128, 1])), deps=deps, dma=True)

        def absrun(src_bf, mxcol, mncol, tmpcol, deps, junk=None):
            nfd = src_bf.shape[-1]
            j = junk if junk is not None else lnj[:, 0:nfd]
            t1 = ts("vector", j, src_bf, 1.0, None, ALU.mult,
                    ALU.max, accum=tmpcol, deps=deps)
            t2 = tt("vector", mxcol, mxcol, tmpcol, ALU.max, deps=[t1])
            t3 = ts("vector", j, src_bf, 1.0, None, ALU.mult,
                    ALU.min, accum=tmpcol, deps=[t2])
            t4 = tt("vector", mncol, mncol, tmpcol, ALU.min, deps=[t3])
            return t4

        A.push()
        zbuf = [A.alloc(I3, F32) for _ in range(2)]
        ubuf0 = A.alloc(I3, F32)
        zqf = A.alloc(I3, F32)
        mxb = A.alloc(96, F32)
        ssb = A.alloc(96, F32)
        sib = A.alloc(96, F32)
        wdqb = [A.alloc(I3, BF16) for _ in range(2)]
        dq_store = []
        last_tr = None
        last_ev = {}
        prev_zq = None
        dq_list = (("up", DC, IC), ("g", DC, IC), ("d", IC, DC)) if "dq" in phases else ()
        for wname, pch, fch in dq_list:
            for c in range(pch):
                par = len(dq_store) % 2
                K, G = fch * 128, fch * 4
                t_ld = P.op("sync", lambda e, s=wZh[wname][:, c * K:(c + 1) * K],
                            d=zbuf[par][:, 0:K]: e.dma_start(out=d, in_=s),
                            deps=[dq_store[-2] if len(dq_store) >= 2 else None],
                            dma=True)
                t_mx = P.op("vector", lambda e,
                            i=zbuf[par][:, 0:K].rearrange("p (g j) -> p g j", j=32),
                            o=mxb[:, 0:G]: e.reduce_max(
                                o, i, axis=AX.X, apply_absolute_value=True),
                            deps=[t_ld], sig=True)
                t_s = ts("vector", ssb[:, 0:G], mxb[:, 0:G], 1.0 / 7.0, 1e-12,
                         ALU.mult, ALU.max, deps=[t_mx, prev_zq])
                t_si = P.op("vector", lambda e, i=ssb[:, 0:G], o=sib[:, 0:G]:
                            e.reciprocal(o, i), deps=[t_s], sig=True)
                t_u = tt("vector",
                         ubuf0[:, 0:K].rearrange("p (g j) -> p g j", j=32),
                         zbuf[par][:, 0:K].rearrange("p (g j) -> p g j", j=32),
                         sib[:, 0:G].unsqueeze(2).broadcast_to([128, G, 32]),
                         ALU.mult, deps=[t_si, last_tr, prev_zq])
                t_q = ts("gpsimd", ubuf0[:, 0:K], ubuf0[:, 0:K], MAGIC, MAGIC,
                         ALU.add, ALU.subtract, deps=[t_u])
                t_zq = tt("gpsimd",
                          zqf[:, 0:K].rearrange("p (g j) -> p g j", j=32),
                          ubuf0[:, 0:K].rearrange("p (g j) -> p g j", j=32),
                          ssb[:, 0:G].unsqueeze(2).broadcast_to([128, G, 32]),
                          ALU.mult, deps=[t_q, last_tr])
                prev_zq = t_zq
                ev_toks = []
                for fb0 in range(0, fch, 4):
                    nsl = min(4, fch - fb0)
                    slot = (fb0 // 4) % 4
                    bk_i = 4 + slot
                    for sl in range(nsl):
                        fb = fb0 + sl
                        dep_tr = [t_zq]
                        if sl == 0:
                            dep_tr.append(last_ev.get(slot))
                        tk_ = P.op("tensor", lambda e,
                                   o=bank(bk_i)[:, sl * 128:(sl + 1) * 128],
                                   i=zqf[:, fb * 128:(fb + 1) * 128]:
                                   e.transpose(o, i, idf),
                                   deps=dep_tr, sig=(sl == nsl - 1))
                    last_tr = tk_
                    t_ev = P.op("scalar", lambda e,
                                s=bank(bk_i)[:, 0:nsl * 128],
                                d=wdqb[par][:, fb0 * 128:(fb0 + nsl) * 128]:
                                e.copy(d, s),
                                deps=[tk_, dq_store[-2] if len(dq_store) >= 2
                                      else None], sig=True)
                    last_ev[slot] = t_ev
                    ev_toks.append(t_ev)
                t_st = P.op("gpsimd", lambda e,
                            d=wdq_h[wname][:, c * K:(c + 1) * K],
                            s=wdqb[par][:, 0:K]: e.dma_start(out=d, in_=s),
                            deps=ev_toks, dma=True)
                dq_store.append(t_st)
        tok_dq = dq_store[-1] if dq_store else tok_const
        A.pop()
        run_attn = [b for b in range(B) if f"attn{b}" in phases]

        A.push()
        qT = A.alloc(DC * S, BF16)
        kT = A.alloc(DC * S, BF16)
        vaug = A.alloc(NT * H * VROW, BF16)
        xqtok = A.alloc(D, BF16)
        xqtr = A.alloc(D, BF16)
        A.push()
        wb3 = [A.alloc(DC * D, BF16) for _ in range(3)]
        xlt = [A.alloc(DC * 128, BF16) for _ in range(2)]
        qkt = [A.alloc(D, BF16) for _ in range(2)]
        rtm = [A.alloc(D, BF16) for _ in range(2)]
        rok = [A.alloc(D, BF16) for _ in range(2)]
        ropc = [A.alloc(512, BF16) for _ in range(2)]
        rops = [A.alloc(512, BF16) for _ in range(2)]
        sqb = A.alloc(1024, BF16)
        proj_mark = A.off
        A.pop()
        A.push()
        wo_sb = A.alloc(DC * D, BF16)
        ptil = [A.alloc(4096, BF16)]
        otp = [A.alloc(8 * SB, BF16) for _ in range(2)]
        lcol = A.alloc((H // 2) * SB, BF16, rows=97)
        lrowf = A.alloc(SB, F32, rows=16)
        lrecf = A.alloc(SB, F32, rows=16)
        lrecb = A.alloc(SB, BF16, rows=16)
        rexp = A.alloc(SB, BF16)
        attnT = A.alloc(DC * SB, BF16)
        attok = A.alloc(4 * D, BF16)
        A.pop()

        vaug_r = vaug.rearrange("p (t h j) -> p t h j", h=H, j=VROW)
        qT_r = qT.rearrange("p (c t) -> p c t", t=S)
        kT_r = kT.rearrange("p (c t) -> p c t", t=S)
        wb3_r = [w.rearrange("p (c e) -> p c e", e=D) for w in wb3]
        wo_r = wo_sb.rearrange("p (c e) -> p c e", e=D)

        t_init = P.op("vector", lambda e: e.memset(vaug_r[:, :, :, 64:65], 1.0),
                      sig=True)
        t_am = P.op("vector", lambda e: e.memset(amax[:, 0:1], -3.0e38), sig=True)
        t_am2 = P.op("vector", lambda e: e.memset(amax[:, 2:3], 3.0e38), sig=True)

        tok_out_stores = []
        tok_phase = tok_dq
        war = {}

        for b in run_attn:
            tw3 = []
            for si, wk_ in enumerate(("wqT", "wkT", "wvT")):
                tw3.append(P.op("gpsimd", lambda e, w=w_h[wk_], d=wb3[si]:
                                e.dma_start(out=d, in_=w[:, :]),
                                deps=[tok_phase], dma=True))
            t_nrm0 = P.op("vector", lambda e: e.memset(amax[:, 6:8], 0.0),
                          deps=war.get("nrmD", []), sig=True)
            tq = tk = tv_ = None
            bank_guard = {}
            for tt_i in range(NT):
                par = tt_i % 2
                xt = xf32[par]
                tl = P.op("sync", lambda e, b=b, t=tt_i, d=xt: e.dma_start(
                    out=d, in_=xh[b, t * 128:(t + 1) * 128, :]),
                    deps=[tok_phase] + war.get(("xf", par), []), dma=True)
                tcv = P.op("sync", lambda e, t=tt_i, d=ropc[par]:
                           e.dma_start(out=d, in_=ropech[t * 128:(t + 1) * 128, :]),
                           deps=[war.get(("rope", par))], dma=True)
                tsv = P.op("sync", lambda e, t=tt_i, d=rops[par]:
                           e.dma_start(out=d, in_=ropesh[t * 128:(t + 1) * 128, :]),
                           deps=[war.get(("rope", par))], dma=True)
                mn, rs, tst, tsq1_ = ln_stats(xt, par, [tl])
                ta = ts("vector", lntmp[par], xt, mn, rs, ALU.subtract, ALU.mult,
                        deps=[tst] + war.get(("lntmp", par), []))
                war[("xf", par)] = [ta, tsq1_]
                ttr = P.op("sync", lambda e, src=lntmp[par], d=xlt[par]: e.dma_start(
                    out=d.rearrange("p (c q) -> p c q", q=128),
                    in_=src.rearrange("p (c q) -> p c q", q=128), transpose=True),
                    deps=[ta] + war.get(("xlt", par), []), dma=True)
                war[("lntmp", par)] = [ttr]
                xlt_r = xlt[par].rearrange("p (c q) -> p c q", q=128)
                xlt_toks = []
                for ki, kind in enumerate(("q", "k", "v")):
                    mm = []
                    for ec in range(2):
                        bk_i = 2 * ki + ec
                        dep0 = [ttr, tw3[ki], bank_guard.get(bk_i)]
                        for dc in range(DC):
                            P.op("tensor", lambda e, o=bank(bk_i),
                                 l=xlt_r[:, dc, :],
                                 r=wb3_r[ki][:, dc, ec * 512:(ec + 1) * 512],
                                 s=(dc == 0): e.matmul(o, l, r, start=s, stop=False),
                                 deps=dep0 if dc == 0 else ())
                        tm = P.op("tensor", lambda e, o=bank(bk_i),
                                  r=bsb["b" + kind[0]][:, ec * 512:(ec + 1) * 512]:
                                  e.matmul(o, ones1, r, start=False, stop=True),
                                  sig=True)
                        mm.append(tm)
                        xlt_toks.append(tm)
                    if kind == "v":
                        for ec in range(2):
                            tv = ts("vector", vaug_r[:, tt_i, ec * 8:(ec + 1) * 8, 0:64],
                                    bank(4 + ec).rearrange("p (h j) -> p h j", j=64),
                                    1.0, None, ALU.mult, deps=[mm[ec], t_init])
                            bank_guard[4 + ec] = tv
                            tv_ = tv
                    else:
                        qk = qkt[par]
                        ev = []
                        for ec in range(2):
                            bk_i = 2 * ki + ec
                            tvv = ts("vector", qk[:, ec * 512:(ec + 1) * 512],
                                     bank(bk_i), 1.0, None, ALU.mult,
                                     deps=[mm[ec]] + war.get(("qkt", par), []))
                            bank_guard[bk_i] = tvv
                            ev.append(tvv)
                        qk_r = qk.rearrange("p (h j) -> p h j", j=64)
                        x1v, x2v = qk_r[:, :, 0:32], qk_r[:, :, 32:64]
                        rkey = "lntmp" if kind == "q" else "rok"
                        ro = lntmp[par] if kind == "q" else rok[par]
                        ro_r = ro.rearrange("p (h j) -> p h j", j=64)
                        o1, o2 = ro_r[:, :, 0:32], ro_r[:, :, 32:64]
                        rt_r = rtm[par].rearrange("p (h j) -> p h j", j=64)
                        u1, u2 = rt_r[:, :, 0:32], rt_r[:, :, 32:64]
                        cv = ropc[par].rearrange("p (h j) -> p h j", j=32)
                        sv = rops[par].rearrange("p (h j) -> p h j", j=32)
                        dr = ev + [tcv, tsv] + war.get(("rtm", par), [])
                        r1 = tt("vector", u1, x1v, cv, ALU.mult, deps=dr)
                        r2 = tt("vector", u2, x2v, sv, ALU.mult, deps=dr)
                        r3 = tt("vector", o1, u1, u2, ALU.subtract,
                                deps=[r1, r2] + war.get((rkey, par), []))
                        r4 = tt("vector", u1, x1v, sv, ALU.mult, deps=[r3])
                        r5 = tt("vector", u2, x2v, cv, ALU.mult, deps=[r3])
                        r6 = tt("vector", o2, u1, u2, ALU.add, deps=[r4, r5])
                        dstT = qT_r if kind == "q" else kT_r
                        ttr2 = P.op("sync", lambda e, src=ro, t=tt_i, d=dstT:
                                    e.dma_start(
                                        out=d[:, :, t * 128:(t + 1) * 128],
                                        in_=src.rearrange("p (c q) -> p c q", q=128),
                                        transpose=True), deps=[r6], dma=True)
                        nacc = amax[:, 8 + 2 * ki + par:9 + 2 * ki + par]
                        tsqn = P.op("scalar", lambda e, i=ro, c=nacc:
                                    e.activation(lnj[:, 0:D], i, AF.Square,
                                                 accum_out=c),
                                    deps=[r6] + war.get(("nacc", ki, par), []),
                                    sig=True)
                        nmg = tt("vector", amax[:, 6 + ki:7 + ki],
                                 amax[:, 6 + ki:7 + ki], nacc, ALU.max,
                                 deps=[tsqn, t_nrm0] + war.get(("nrun", ki), []))
                        war[("nacc", ki, par)] = [nmg]
                        war[("nrun", ki)] = [nmg]
                        war[(rkey, par)] = [ttr2, tsqn]
                        war[("rope", par)] = r6
                        war[("rtm", par)] = [r6]
                        war[("qkt", par)] = [r4, r5]
                        if kind == "q":
                            tq = ttr2
                        else:
                            tk = ttr2
                war[("xlt", par)] = xlt_toks
            for p_ in range(3):
                war[("bankpair", p_)] = [bank_guard.get(2 * p_),
                                         bank_guard.get(2 * p_ + 1)]
            tok_wo = P.op("gpsimd", lambda e: e.dma_start(out=wo_sb, in_=w_h["woT"][:, :]),
                          deps=[tv_], dma=True)

            tnq = P.op("gpsimd", lambda e: e.dma_start(
                out=ncol_d[0:1, :], in_=amax[:, 6:7]),
                deps=war.get(("nrun", 0), []) + [WAR0.get("colD")], dma=True)
            tnk = P.op("gpsimd", lambda e: e.dma_start(
                out=ncol_d[1:2, :], in_=amax[:, 7:8]),
                deps=war.get(("nrun", 1), []), dma=True)
            war["nrmD"] = [tnq, tnk]
            tn4 = P.op("sync", lambda e: e.dma_start(
                out=smallf[0:2, 64:192], in_=ncol_d[:, :]),
                deps=[tnq, tnk, WAR0.get("colD")], dma=True)
            tn5 = P.op("vector", lambda e: e.reduce_max(
                smallf[0:2, 42:43], smallf[0:2, 64:192], axis=AX.X),
                deps=[tn4], sig=True)
            tn7 = P.op("scalar", lambda e: e.activation(
                smallf[0:2, 43:44], smallf[0:2, 42:43], AF.Sqrt),
                deps=[tn5], sig=True)
            tn3 = P.op("gpsimd", lambda e: e.dma_start(
                out=col_d[0:1, 0:2], in_=smallf[0:2, 43:44]),
                deps=[tn7, WAR0.get("colD")], dma=True)
            tn4b = P.op("sync", lambda e: e.dma_start(
                out=smallf[0:1, 48:50], in_=col_d[0:1, 0:2]),
                deps=[tn3], dma=True)
            WAR0["colD"] = tn4b
            tn9 = tt("vector", smallf[0:1, 56:57], smallf[0:1, 48:49],
                     smallf[0:1, 49:50], ALU.mult, deps=[tn4b])
            tn10 = ts("vector", smallf[0:1, 57:58], smallf[0:1, 56:57], -0.125,
                      None, ALU.mult, deps=[tn9])
            tn11 = P.op("gpsimd", lambda e: e.dma_start(
                out=sc_d[0:1, 16:17], in_=smallf[0:1, 57:58]),
                deps=[tn10, war.get("scD16")], dma=True)
            tok_m0 = bcast_col(sc_d[0:1, 16:17], scol("negm0"),
                               [tn11, war.get("negm0")])
            war["scD16"] = tok_m0

            def emit_pipe(sb_i, mid=None):
                buf = sb_i % 2
                evs = {}
                for h in range(H):
                    if mid and h in (2, 5, 8, 11):
                        mid()
                    hp, odd = h // 2, h % 2
                    r0 = 64 * odd
                    tpos = (64, 0) if odd else (0, 0)
                    pairbanks = ((0, 1), (2, 3), (4, 5))
                    expt = {}
                    mmt = {}
                    pv = None
                    for pr in range(NT // 2 + 1):
                        if pr < NT // 2:
                            bks = pairbanks[pr % 3]
                            for j in range(2):
                                tc = 2 * pr + j
                                depq = [tv_, expt.get(pr - 3)] + (
                                    war.get(("bankpair", pr % 3), [])
                                    if pr < 3 else [])
                                mmt[pr] = P.op("tensor", lambda e,
                                    o=bank(bks[j], rows=128),
                                    l=kT_r[r0:r0 + 64, hp, tc * 128:(tc + 1) * 128],
                                    r=qT_r[r0:r0 + 64, hp, sb_i * SB:(sb_i + 1) * SB],
                                    tp=tpos: e.matmul(o, l, r, start=True,
                                                      stop=True, tile_position=tp),
                                    deps=depq if j == 0 else (), sig=(j == 1))
                            expt[pr] = P.op("scalar", lambda e,
                                o=ptil[0][:, (pr % 4) * 1024:(pr % 4 + 1) * 1024],
                                i=bank(bks[0], n=2),
                                : e.activation(o, i, AF.Exp, bias=scol("negm0"),
                                               scale=0.125),
                                deps=[mmt[pr], tok_m0, war.get(("ptil", pr % 4))],
                                sig=True)
                            war[("bankpair", pr % 3)] = [expt[pr]]
                        if pr >= 1:
                            po = pr - 1
                            for j in range(2):
                                tc = 2 * po + j
                                depv = [expt[po]] + (war.get("pv_evac", [])
                                                     if tc == 0 else [])
                                pv = P.op("tensor", lambda e, o=bank(6, rows=65),
                                    l=vaug_r[:, tc, h, 0:65],
                                    r=ptil[0][:, (po % 4) * 1024 + j * 512:
                                              (po % 4) * 1024 + (j + 1) * 512],
                                    s=(tc == 0), p=(tc == NT - 1):
                                    e.matmul(o, l, r, start=s, stop=p),
                                    deps=depv if (j == 0 or tc == 0) else (),
                                    sig=(tc == NT - 1))
                            war[("ptil", po % 4)] = pv
                    tev = vcopy(otp[buf][r0:r0 + 64, hp * SB:(hp + 1) * SB],
                                bank(6, rows=64),
                                deps=[pv, war.get(("otp", buf, hp))])
                    lr = r0 + 32 * buf
                    tl = vcopy(lcol[lr:lr + 1, hp * SB:(hp + 1) * SB],
                               psap[64:65, 6 * 512:7 * 512],
                               deps=[pv] + war.get(("lcol", buf), []))
                    war["pv_evac"] = [tev, tl]
                    war["last_exp"] = expt[NT // 2 - 1]
                    evs[h] = (tev, tl)
                return [(hp, evs[2 * hp][0], evs[2 * hp + 1][0],
                         evs[2 * hp][1], evs[2 * hp + 1][1])
                        for hp in range(H // 2)]

            def emit_lchain(sb_i, pair_toks):
                buf = sb_i % 2
                tg1 = P.op("gpsimd", lambda e: e.dma_start(
                    out=lcol_d[0:1, :], in_=lcol[32 * buf:32 * buf + 1, :]),
                    deps=[t[3] for t in pair_toks] + [war.get("lrowf")], dma=True)
                tg2 = P.op("gpsimd", lambda e: e.dma_start(
                    out=lcol_d[1:2, :], in_=lcol[64 + 32 * buf:65 + 32 * buf, :]),
                    deps=[t[4] for t in pair_toks], dma=True)
                tg3 = P.op("gpsimd", lambda e: e.dma_start(
                    out=lrowf, in_=lcol_d[:, :].rearrange("r (h j) -> (r h) j", j=SB)),
                    deps=[tg1, tg2], dma=True)
                war[("lcol", buf)] = [tg1, tg2]
                tg4 = P.op("vector", lambda e: e.reciprocal(lrecf, lrowf),
                           deps=[tg3], sig=True)
                war["lrowf"] = tg4
                tg5 = ts("vector", lrecb, lrecf, 1.0, None, ALU.mult, deps=[tg4])
                tg6 = P.op("gpsimd", lambda e: e.dma_start(
                    out=lrec_d[:, :], in_=lrecb),
                    deps=[tg5, war.get("lrecd")], dma=True)
                pre0a = P.op("sync", lambda e: e.dma_start(
                    out=rexp[0:64, :],
                    in_=lrec_d[0:1, :].broadcast_to([64, SB])),
                    deps=[tg6, war.get("rexp")], dma=True)
                pre0b = P.op("sync", lambda e: e.dma_start(
                    out=rexp[64:128, :],
                    in_=lrec_d[8:9, :].broadcast_to([64, SB])),
                    deps=[tg6, war.get("rexp")], dma=True)
                return tg6, pre0a, pre0b

            def emit_tail(sb_i, pair_toks, lch):
                buf = sb_i % 2
                tg6, pre0a, pre0b = lch
                div_toks = []
                rexp_guard = None
                for hp, tev1, tev2, _, _ in pair_toks:
                    if hp == 0:
                        tr1a, tr1 = pre0a, pre0b
                    else:
                        tr1a = P.op("sync", lambda e, hp=hp: e.dma_start(
                            out=rexp[0:64, :],
                            in_=lrec_d[hp:hp + 1, :].broadcast_to([64, SB])),
                            deps=[tg6, rexp_guard], dma=True)
                        tr1 = P.op("sync", lambda e, hp=hp: e.dma_start(
                            out=rexp[64:128, :],
                            in_=lrec_d[8 + hp:9 + hp, :].broadcast_to([64, SB])),
                            deps=[tg6, rexp_guard], dma=True)
                    td = tt("gpsimd", otp[buf][:, hp * SB:(hp + 1) * SB],
                            otp[buf][:, hp * SB:(hp + 1) * SB], rexp, ALU.mult,
                            deps=[tr1a, tr1, tev1, tev2])
                    rexp_guard = td
                    div_toks.append(td)
                war["rexp"] = rexp_guard
                war["lrecd"] = div_toks[-1]
                ev2 = []
                bank6_guard = war.get("bank7")
                for fc in range(DC):
                    for pp in range(DC):
                        tmo = P.op("tensor", lambda e, o=bank(7),
                                   l=wo_r[:, pp, fc * 128:(fc + 1) * 128],
                                   r=otp[buf][:, pp * SB:(pp + 1) * SB],
                                   s=(pp == 0), p=(pp == DC - 1):
                                   e.matmul(o, l, r, start=s, stop=p),
                                   deps=[div_toks[-1], tok_wo, bank6_guard]
                                   if pp == 0 else (), sig=(pp == DC - 1))
                    tev = ts("vector", attnT[:, fc * SB:(fc + 1) * SB], bank(7),
                             bo_sb[:, fc:fc + 1], None, ALU.add,
                             deps=[tmo, war.get("attnT")])
                    bank6_guard = tev
                    ev2.append(tev)
                war["bank7"] = bank6_guard
                for hp in range(DC):
                    war[("otp", buf, hp)] = ev2[-1]
                ttr3 = None
                for fc in range(DC):
                    ttr3 = P.op("sync", lambda e, fc=fc: e.dma_start(
                        out=attok.rearrange("p (w f q) -> p w f q", f=DC, q=128
                                            )[:, :, fc, :],
                        in_=attnT[:, fc * SB:(fc + 1) * SB],
                        transpose=True),
                        deps=[ev2[fc]] + war.get("attok", []), dma=True)
                war["attnT"] = ttr3
                return ttr3

            def emit_tail_ln(sb_i, ttr3):
                for tw in range(4):
                    gt = sb_i * 4 + tw
                    par = gt % 2
                    atile = attok[:, tw * D:(tw + 1) * D]
                    mn, rs, tst, tsq_ = ln_stats(atile, par, [ttr3])
                    t_ln2 = ts("vector", lntmp[par], atile, mn, rs,
                               ALU.subtract, ALU.mult,
                               deps=[tst] + war.get(("lntmp", par), []))
                    t_xl = P.op("sync", lambda e, d=xf32[par], b=b, g=gt:
                                e.dma_start(out=d, in_=xh[b, g * 128:(g + 1) * 128, :]),
                                deps=war.get(("xf", par), []), dma=True)
                    t_x1 = tt("vector", x1t, xf32[par], lntmp[par], ALU.add,
                              deps=[t_xl, t_ln2] + war.get("x1t", []))
                    war[("xf", par)] = [t_x1]
                    t_x1s = P.op("gpsimd", lambda e, b=b, g=gt: e.dma_start(
                        out=x1_scr[b * NT + g], in_=x1t), deps=[t_x1], dma=True)
                    mn2, rs2, tst2, tsq2_ = ln_stats(x1t, par, [t_x1])
                    t_xq = ts("vector", xqtok, x1t, mn2, rs2, ALU.subtract,
                              ALU.mult, deps=[tst2, war.get("xqtok")])
                    war["x1t"] = [t_x1s, tsq2_]
                    t_ab = absrun(xqtok, amax[:, 0:1], amax[:, 2:3],
                                  amax[:, 4:5], [t_xq, t_am, t_am2])
                    t_tr4 = P.op("sync", lambda e: e.dma_start(
                        out=xqtr.rearrange("p (c q) -> p c q", q=128),
                        in_=xqtok.rearrange("p (c q) -> p c q", q=128),
                        transpose=True), deps=[t_xq, war.get("xqtr")], dma=True)
                    war["xqtok"] = t_tr4
                    t_xqs = P.op("gpsimd", lambda e, b=b, g=gt: e.dma_start(
                        out=xq_scr.rearrange("p (c t) -> p c t", t=B * S)[
                            :, :, (b * S + g * 128):(b * S + (g + 1) * 128)],
                        in_=xqtr.rearrange("p (c q) -> p c q", q=128)),
                        deps=[t_tr4], dma=True)
                    war["xqtr"] = t_xqs
                    war[("lntmp", par)] = [t_x1]
                    war["attok"] = [tsq_, t_ln2]
                    lnres["tok"] = t_xqs
                    yield

            pend = []
            lnq = []
            lnres = {"tok": None}

            def _lnstep(n):
                while lnq and n > 0:
                    try:
                        next(lnq[0])
                        n -= 1
                    except StopIteration:
                        lnq.pop(0)

            def _mid():
                _lnstep(1)

            for sb_i in range(NSB):
                pt = emit_pipe(sb_i, _mid)
                if pend:
                    sbp, ptp, lchp = pend.pop(0)
                    t3_ = emit_tail(sbp, ptp, lchp)
                    lnq.append(emit_tail_ln(sbp, t3_))
                pend.append((sb_i, pt, emit_lchain(sb_i, pt)))
            sbp, ptp, lchp = pend.pop(0)
            _lnstep(99)
            t3_ = emit_tail(sbp, ptp, lchp)
            for _ in emit_tail_ln(sbp, t3_):
                pass
            tok_phase = lnres["tok"]
            war["negm0"] = war.get("last_exp")
        tok_attn_done = tok_phase
        A.pop()

        A.push()
        wff = A.alloc(DC * I3, BF16)
        mxc = {k: A.alloc(4, F32) for k in ("up", "gate", "g2", "o")}
        acc_t = A.alloc(2, F32)
        A.push()
        wff2 = A.alloc(DC * I3, BF16)
        xqb = [A.alloc(DC * 512, BF16) for _ in range(2)]
        xqq = [A.alloc(DC * 512, BF16) for _ in range(2)]
        qtmp = A.alloc(DC * 512, F32)
        spb = [A.alloc(2048, BF16) for _ in range(2)]
        A.pop()
        A.push()
        gb = [A.alloc(4096, BF16) for _ in range(2)]
        g2b = [A.alloc(4096, BF16) for _ in range(2)]
        ub = [A.alloc(4096, BF16) for _ in range(2)]
        sgb = [A.alloc(4096, BF16) for _ in range(2)]
        ftmp = [A.alloc(4096, F32) for _ in range(2)]
        A.pop()
        A.push()
        ob = A.alloc(IC * 512, BF16)
        oq = A.alloc(IC * 512, BF16)
        dnT = A.alloc(DC * 512, BF16)
        dntok = A.alloc(4 * D, BF16)
        ftmp4 = [A.alloc(2048, F32) for _ in range(2)]
        A.pop()

        def ar(idx, cols, names, deps):
            toks = []
            guard = list(deps) + [WAR0.get("ar")]
            for i, (mxcol, mncol) in enumerate(cols):
                tn = ts("vector", smallf[:, 60:61], mncol, -1.0, None, ALU.mult,
                        deps=guard)
                tm = tt("vector", smallf[:, 61:62], mxcol, smallf[:, 60:61],
                        ALU.max, deps=[tn])
                tcx, _ = collapse2(smallf[:, 61:62], [tm], i)
                toks.append(P.op("gpsimd", lambda e, i=i, idx=idx: e.dma_start(
                    out=cc_io[idx][0][0:1, i:i + 1], in_=sc_d[0:1, i:i + 1]),
                    deps=[tcx], dma=True))
                guard = [tcx]
            tcc_ = P.op("gpsimd", lambda e, idx=idx: e.collective_compute(
                "AllReduce", ALU.max, replica_groups=[list(range(NCORES))],
                ins=[cc_io[idx][0].opt()], outs=[cc_io[idx][1].opt()]),
                deps=toks, cc=True)
            last = None
            for i, (sn, isn) in enumerate(names):
                tbx = bcast_col(cc_io[idx][1][0:1, i:i + 1], scol("gtmp"),
                                [tcc_, last])
                tsa = ts("vector", scol(sn), scol("gtmp"), 1.0 / QS, 1e-12,
                         ALU.mult, ALU.max, deps=[tbx])
                last = P.op("vector", lambda e, sn=sn, isn=isn: e.reciprocal(
                    scol(isn), scol(sn)), deps=[tsa], sig=True)
            WAR0["ar"] = last
            return last

        def _ffn_body():
            tok_sx = ar(0, [(amax[:, 0:1], amax[:, 2:3])], [("sx", "isx")],
                        [tok_attn_done])

            t_wu = P.op("sync", lambda e: e.dma_start(out=wff, in_=wdq_h["up"][:, :]),
                        deps=[tok_dq, tok_attn_done], dma=True)
            t_wg = P.op("sync", lambda e: e.dma_start(out=wff2, in_=wdq_h["g"][:, :]),
                        deps=[tok_dq, tok_attn_done], dma=True)
            for k in ("up", "gate", "g2", "o"):
                P.op("vector", lambda e, k=k: e.memset(mxc[k][:, 0:1], -3.0e38), sig=True)
                P.op("vector", lambda e, k=k: e.memset(mxc[k][:, 1:2], 3.0e38), sig=True)
            wff_r = wff.rearrange("p (c i) -> p c i", i=I3)
            wff2_r = wff2.rearrange("p (c i) -> p c i", i=I3)
            xq_scr_r = xq_scr.rearrange("p (c t) -> p c t", t=B * S)

            last_sp = {}
            bank_g = {}
            last_tg = [None, None]
            for tg_i in range(TG):
                par = tg_i % 2
                tld = P.op("sync", lambda e, t=tg_i, d=xqb[par]: e.dma_start(
                    out=d.rearrange("p (c q) -> p c q", q=512),
                    in_=xq_scr_r[:, :, t * 512:(t + 1) * 512]),
                    deps=[tok_sx, last_tg[par]], dma=True)
                tq1 = ts("vector", qtmp, xqb[par], scol("isx"), MAGIC, ALU.mult,
                         ALU.add, deps=[tld, tok_sx, war.get("qtmp")])
                tq2 = ts("vector", xqq[par], qtmp, MAGIC, scol("sx"), ALU.subtract,
                         ALU.mult, deps=[tq1, tok_sx])
                war["qtmp"] = tq2
                last_tg[par] = tq2
                xqr = xqq[par].rearrange("p (c q) -> p c q", q=512)
                for wi, (wr, scr, mx) in enumerate(
                        ((wff_r, spill["up"], mxc["up"]),
                         (wff2_r, spill["gate"], mxc["gate"]))):
                    for icg in range(IC // 4):
                        evg = []
                        sp_par = (tg_i * 2 + icg) % 2
                        for s4 in range(4):
                            ic = icg * 4 + s4
                            bk_i = s4 % 2 + 2 * wi
                            for dc in range(DC):
                                tmf = P.op("tensor", lambda e, o=bank(bk_i),
                                           l=wr[:, dc, ic * 128:(ic + 1) * 128],
                                           r=xqr[:, dc, :], s=(dc == 0),
                                           p=(dc == DC - 1):
                                           e.matmul(o, l, r, start=s, stop=p),
                                           deps=[tq2, t_wu, t_wg, bank_g.get(bk_i)]
                                           if dc == 0 else (), sig=(dc == DC - 1))
                            tev = P.op("scalar", lambda e,
                                       o=spb[sp_par][:, s4 * 512:(s4 + 1) * 512],
                                       i=bank(bk_i): e.copy(o, i),
                                       deps=[tmf, last_sp.get(sp_par)], sig=True)
                            bank_g[bk_i] = tev
                            evg.append(tev)
                            ta = absrun(spb[sp_par][:, s4 * 512:(s4 + 1) * 512],
                                        mx[:, 0:1], mx[:, 1:2], acc_t[:, 0:1], [tev])
                            evg.append(ta)
                        tst_ = P.op("gpsimd", lambda e, t=tg_i, icg=icg, scr=scr,
                                    sp=sp_par: e.dma_start(
                                        out=scr[t, icg * 4:(icg + 1) * 4].transpose([1, 0, 2]),
                                        in_=spb[sp].rearrange("p (a q) -> p a q", q=512)),
                                    deps=evg, dma=True)
                        last_sp[sp_par] = tst_
            tok_ar2 = ar(1, [(mxc["up"][:, 0:1], mxc["up"][:, 1:2]),
                             (mxc["gate"][:, 0:1], mxc["gate"][:, 1:2])],
                         [("sup", "isup"), ("sg", "isg")],
                         [last_sp[0], last_sp[1]])

            prevs = {}
            p2_last = None
            for gidx in range(TG * (IC // 8)):
                tg_i, icg = divmod(gidx, IC // 8)
                par = gidx % 2
                tld = P.op("sync", lambda e, t=tg_i, icg=icg, d=gb[par]: e.dma_start(
                    out=d.rearrange("p (a q) -> p a q", q=512),
                    in_=spill["gate"][t, icg * 8:(icg + 1) * 8].transpose([1, 0, 2])),
                    deps=[tok_ar2, prevs.get(("st", par))], dma=True)
                m1 = ts("vector", ftmp[par], gb[par], scol("isg"), MAGIC, ALU.mult,
                        ALU.add, deps=[tld, prevs.get(("ft", par))])
                m2 = ts("vector", gb[par], ftmp[par], MAGIC, scol("sg"), ALU.subtract,
                        ALU.mult, deps=[m1, tok_ar2])
                m3 = P.op("scalar", lambda e, i=gb[par], o=sgb[par]: e.activation(
                    o, i, AF.Sigmoid), deps=[m2, prevs.get(("sg", par))], sig=True)
                m4 = ts("vector", ftmp[par], sgb[par], 65536.0, MAGIC, ALU.mult,
                        ALU.add, deps=[m3, m2])
                m5 = ts("vector", sgb[par], ftmp[par], MAGIC, 65535.0, ALU.subtract,
                        ALU.min, deps=[m4])
                prevs[("ft", par)] = m5
                m6 = tt("vector", g2b[par], gb[par], sgb[par], ALU.mult,
                        deps=[m5, prevs.get(("g2", par))])
                prevs[("sg", par)] = m6
                m7 = absrun(g2b[par], mxc["g2"][:, 0:1], mxc["g2"][:, 1:2],
                            acc_t[:, 0:1], [m6], junk=gb[par])
                tst_ = P.op("gpsimd", lambda e, t=tg_i, icg=icg, par=par: e.dma_start(
                    out=spill["g2"][t, icg * 8:(icg + 1) * 8].transpose([1, 0, 2]),
                    in_=g2b[par].rearrange("p (a q) -> p a q", q=512)),
                    deps=[m7, m6], dma=True)
                prevs[("g2", par)] = tst_
                prevs[("st", par)] = tst_
                p2_last = tst_
            tok_ar3 = ar(2, [(mxc["g2"][:, 0:1], mxc["g2"][:, 1:2])],
                         [("sg2", "isg2")],
                         [prevs.get(("st", 0)), prevs.get(("st", 1))])

            prevs = {}
            for gidx in range(TG * (IC // 8)):
                tg_i, icg = divmod(gidx, IC // 8)
                par = gidx % 2
                tld1 = P.op("sync", lambda e, t=tg_i, icg=icg, d=gb[par]: e.dma_start(
                    out=d.rearrange("p (a q) -> p a q", q=512),
                    in_=spill["g2"][t, icg * 8:(icg + 1) * 8].transpose([1, 0, 2])),
                    deps=[tok_ar3, prevs.get(("st", par))], dma=True)
                tld2 = P.op("sync", lambda e, t=tg_i, icg=icg, d=ub[par]: e.dma_start(
                    out=d.rearrange("p (a q) -> p a q", q=512),
                    in_=spill["up"][t, icg * 8:(icg + 1) * 8].transpose([1, 0, 2])),
                    deps=[tok_ar3, prevs.get(("st", par))], dma=True)
                m1 = ts("vector", ftmp[par], gb[par], scol("isg2"), MAGIC, ALU.mult,
                        ALU.add, deps=[tld1, prevs.get(("ft", par))])
                m2 = ts("vector", gb[par], ftmp[par], MAGIC, scol("sg2"), ALU.subtract,
                        ALU.mult, deps=[m1, tok_ar3])
                m3 = ts("vector", ftmp[par], ub[par], scol("isup"), MAGIC, ALU.mult,
                        ALU.add, deps=[tld2, m2])
                m4 = ts("vector", ub[par], ftmp[par], MAGIC, scol("sup"), ALU.subtract,
                        ALU.mult, deps=[m3, tok_ar2])
                prevs[("ft", par)] = m4
                m5 = tt("vector", g2b[par], gb[par], ub[par], ALU.mult,
                        deps=[m4, m2, prevs.get(("g2", par))])
                m6 = absrun(g2b[par], mxc["o"][:, 0:1], mxc["o"][:, 1:2],
                            acc_t[:, 0:1], [m5], junk=gb[par])
                tst_ = P.op("gpsimd", lambda e, t=tg_i, icg=icg, par=par: e.dma_start(
                    out=spill["o"][t, icg * 8:(icg + 1) * 8].transpose([1, 0, 2]),
                    in_=g2b[par].rearrange("p (a q) -> p a q", q=512)),
                    deps=[m6, m5], dma=True)
                prevs[("g2", par)] = tst_
                prevs[("st", par)] = tst_
            tok_ar4 = ar(3, [(mxc["o"][:, 0:1], mxc["o"][:, 1:2])], [("so", "iso")],
                         [prevs.get(("st", 0)), prevs.get(("st", 1))])

            t_wd = P.op("sync", lambda e: e.dma_start(out=wff, in_=wdq_h["d"][:, :]),
                        deps=[tok_ar2], dma=True)
            wd_r = wff.rearrange("p (c i) -> p c i", i=D)
            war2 = {}
            for tg_i in range(TG):
                tld = P.op("sync", lambda e, t=tg_i: e.dma_start(
                    out=ob.rearrange("p (a q) -> p a q", q=512),
                    in_=spill["o"][t].transpose([1, 0, 2])),
                    deps=[tok_ar4, war2.get("ob")], dma=True)
                q_toks = []
                qprev = war2.get("oq")
                fprev = None
                for icg in range(IC // 4):
                    fpar = icg % 2
                    m1 = ts("vector", ftmp4[fpar], ob[:, icg * 2048:(icg + 1) * 2048],
                            scol("iso"), MAGIC, ALU.mult, ALU.add,
                            deps=[tld, fprev])
                    m2 = ts("vector", oq[:, icg * 2048:(icg + 1) * 2048], ftmp4[fpar],
                            MAGIC, scol("so"), ALU.subtract, ALU.mult,
                            deps=[m1, qprev, tok_ar4])
                    fprev = m2
                    q_toks.append(m2)
                war2["ob"] = q_toks[-1]
                oqr = oq.rearrange("p (c q) -> p c q", q=512)
                ev3 = []
                bank_pg = {}
                for h2c in range(DC):
                    bk_i = h2c % 4
                    for qc in range(IC):
                        tmf = P.op("tensor", lambda e, o=bank(bk_i),
                                   l=wd_r[:, qc, h2c * 128:(h2c + 1) * 128],
                                   r=oqr[:, qc, :], s=(qc == 0), p=(qc == IC - 1):
                                   e.matmul(o, l, r, start=s, stop=p),
                                   deps=[q_toks[qc // 4], t_wd, bank_pg.get(bk_i),
                                         war2.get(("bank", bk_i))]
                                   if qc % 4 == 0 else (), sig=(qc == IC - 1))
                    tev = ts("vector", dnT[:, h2c * 512:(h2c + 1) * 512], bank(bk_i),
                             1.0 / 65536.0, None, ALU.mult,
                             deps=[tmf, war2.get("dnT")])
                    bank_pg[bk_i] = tev
                    war2[("bank", bk_i)] = tev
                    ev3.append(tev)
                war2["oq"] = ev3[-1]
                ttr5 = None
                for fc in range(DC):
                    ttr5 = P.op("sync", lambda e, fc=fc: e.dma_start(
                        out=dntok.rearrange("p (w f q) -> p w f q", f=DC, q=128
                                            )[:, :, fc, :],
                        in_=dnT[:, fc * SB:(fc + 1) * SB],
                        transpose=True), deps=[ev3[fc]] + [war2.get("dntok")], dma=True)
                war2["dnT"] = ttr5
                last_sq = None
                for tw in range(4):
                    gt = tg_i * 4 + tw
                    b_i, tt_i = gt // NT, gt % NT
                    par = gt % 2
                    atile = dntok[:, tw * D:(tw + 1) * D]
                    mn, rs, tst_, tsq_ = ln_stats(atile, par, [ttr5])
                    t_ln4 = ts("vector", lntmp[par], atile, mn, rs, ALU.subtract,
                               ALU.mult, deps=[tst_] + war.get(("lntmp", par), []))
                    t_xl = P.op("sync", lambda e, d=xf32[par], g=gt: e.dma_start(
                        out=d, in_=x1_scr[g]), deps=war.get(("xf", par), []), dma=True)
                    t_o = tt("vector", x1t, xf32[par], lntmp[par], ALU.add,
                             deps=[t_xl, t_ln4] + war.get("x1t", []))
                    war[("xf", par)] = [t_o]
                    t_os = P.op("gpsimd", lambda e, b=b_i, t=tt_i: e.dma_start(
                        out=outh[b, t * 128:(t + 1) * 128, :], in_=x1t),
                        deps=[t_o], dma=True)
                    war["x1t"] = [t_os]
                    war[("lntmp", par)] = [t_o]
                    last_sq = tsq_
                    tok_out_stores.append(t_os)
                war2["dntok"] = last_sq
            A.pop()

        if "ffn" in phases:
            _ffn_body()

        if not tok_out_stores:
            tok_out_stores.append(tok_attn_done)
        if tap == "ffn4":
            t_dbg = None
            for i, k in enumerate(("up", "gate", "g2", "o")):
                t_dbg = P.op("gpsimd", lambda e, s=spill[k], i=i: e.dma_start(
                    out=dbg[i], in_=s), deps=[tok_out_stores[-1]], dma=True)
            tok_out_stores.append(t_dbg)
        elif tap is not None:
            srcs = {"x1": x1_scr, "xq": xq_scr, "up": spill["up"],
                    "gate": spill["gate"], "g2": spill["g2"], "o": spill["o"],
                    "wupdq": wdq_h["up"], "wgdq": wdq_h["g"], "wddq": wdq_h["d"]}
            t_dbg = P.op("gpsimd", lambda e, s=srcs[tap]: e.dma_start(
                out=dbg, in_=s), deps=[tok_out_stores[-1]], dma=True)
            tok_out_stores.append(t_dbg)

        P.op("gpsimd", lambda e: e.engine_nop(), deps=[tok_out_stores[-1]])

        block = es.enter_context(nc.Block())

        @block.sync
        def _(e):
            P.replay("sync", e, sems)

        @block.gpsimd
        def _(e):
            P.replay("gpsimd", e, sems)

        @block.tensor
        def _(e):
            P.replay("tensor", e, sems)

        @block.vector
        def _(e):
            P.replay("vector", e, sems)

        @block.scalar
        def _(e):
            P.replay("scalar", e, sems)

    build_program.last_plan = P
    return nc


def kernel(**inputs):
    in_maps = _stage(inputs)
    nc = build_program()
    res = run_bass_kernel_spmd(nc, in_maps, core_ids=list(range(NCORES)))
    return np.concatenate(
        [np.asarray(r["out"]) for r in res.results], axis=0).astype(np.float32)


# version 17
# speedup vs baseline: 1.1381x; 1.1381x over previous
import contextlib
import numpy as np
import ml_dtypes
import concourse.bass as bass
import concourse.mybir as mybir
from concourse.bass_utils import run_bass_kernel_spmd

F32 = mybir.dt.float32
BF16 = mybir.dt.bfloat16
U8 = mybir.dt.uint8
AF = mybir.ActivationFunctionType
ALU = mybir.AluOpType
AX = mybir.AxisListType
bf16 = ml_dtypes.bfloat16

MAGIC = 1.5 * 2.0 ** 23
NCORES = 8
B, S, D, H, HD, I3 = 2, 2048, 1024, 16, 64, 3072
NT, DC, IC = S // 128, D // 128, I3 // 128
SB, NSB, TG = 512, S // 512, B * S // 512
VROW = 66
EPS = 1e-5
QS = 32767.0
ENGS = ("sync", "gpsimd", "tensor", "vector", "scalar")
NDMASEM = 8
DMA_QUEUES = ("sync", "gpsimd", "scalar_dma")
DMA_SEMS = tuple(f"{q}{i}" for q in DMA_QUEUES for i in range(NDMASEM))


class Plan:
    def __init__(self):
        self.steps = {e: [] for e in ENGS}
        self.cnt = {}

    def op(self, eng, emit, deps=(), sig=False, dma=False, cc=False):
        waits = {}
        for d in deps:
            if d is None:
                continue
            sn, v = d
            if waits.get(sn, 0) < v:
                waits[sn] = v
        sem, inc, tok = None, 1, None
        if cc:
            sem = "cc"
        elif dma:
            q = "scalar_dma" if eng == "scalar" else eng
            i = self.cnt.get("_q_" + q, 0)
            self.cnt["_q_" + q] = i + 1
            sem, inc = f"{q}{i % NDMASEM}", 16
            prev = self.cnt.get(sem, 0)
            if prev and waits.get(sem, 0) < prev:
                waits[sem] = prev
        elif sig:
            sem = eng
        if sem is not None:
            self.cnt[sem] = self.cnt.get(sem, 0) + inc
            tok = (sem, self.cnt[sem])
        self.steps[eng].append((waits, emit, sem, inc))
        return tok

    def replay(self, eng_name, eng, sems):
        done = {}
        for waits, emit, sem, inc in self.steps[eng_name]:
            for sn in sorted(waits):
                v = waits[sn]
                if (sn == eng_name == "tensor"):
                    continue
                if done.get(sn, 0) >= v:
                    continue
                eng.wait_ge(sems[sn], v)
                done[sn] = v
            ins = emit(eng)
            if sem is not None:
                ins.then_inc(sems[sem], inc)


class Arena:
    def __init__(self, handle, nbytes):
        self.h, self.nbytes, self.off, self.marks = handle, nbytes, 0, []

    def push(self):
        self.marks.append(self.off)

    def pop(self):
        self.off = self.marks.pop()

    def alloc(self, cols, dtype, rows=128):
        nb = cols * mybir.dt.size(dtype)
        off = self.off
        self.off += (nb + 63) & ~63
        assert self.off <= self.nbytes, f"SBUF overflow {self.off}>{self.nbytes}"
        return self.h.ap()[0:rows, off:off + nb].bitcast(dtype)


def _ropetables():
    inv = 1.0 / (10000.0 ** (np.arange(0, HD, 2, dtype=np.float64) / HD))
    fr = np.outer(np.arange(S, dtype=np.float64), inv)
    return (np.tile(np.cos(fr), (1, H)).astype(bf16),
            np.tile(np.sin(fr), (1, H)).astype(bf16))


def _img(w, pchunks):
    r, x = w.shape
    assert r == pchunks * 128
    return np.ascontiguousarray(
        w.reshape(pchunks, 128, x).transpose(1, 0, 2).reshape(128, pchunks * x))


def _z2img(wt, pchunks, fchunks):
    z = wt.reshape(pchunks, 128, fchunks, 128).transpose(0, 3, 2, 1)
    return np.ascontiguousarray(
        z.reshape(pchunks, 128, fchunks * 128).transpose(1, 0, 2).reshape(
            128, pchunks * fchunks * 128))


def _stage(inputs):
    perm = np.concatenate([np.arange(0, HD, 2), np.arange(1, HD, 2)])
    permg = (np.arange(H)[:, None] * HD + perm[None, :]).reshape(-1)
    wq = np.asarray(inputs["L0_attn_Wq"])[permg, :]
    wk = np.asarray(inputs["L0_attn_Wk"])[permg, :]
    wv = np.asarray(inputs["L0_attn_Wv"])
    wo = np.asarray(inputs["L0_attn_Wo"])
    wupT = np.ascontiguousarray(np.asarray(inputs["L0_ffn_w_up"], np.float32).T)
    wgT = np.ascontiguousarray(np.asarray(inputs["L0_ffn_w_gate"], np.float32).T)
    wdT = np.ascontiguousarray(np.asarray(inputs["L0_ffn_w_down"], np.float32).T)
    rc, rs = _ropetables()
    shared = {
        "wqT": _img(np.ascontiguousarray(wq.T), DC).astype(bf16),
        "wkT": _img(np.ascontiguousarray(wk.T), DC).astype(bf16),
        "wvT": _img(np.ascontiguousarray(wv.T), DC).astype(bf16),
        "woT": _img(np.ascontiguousarray(wo.T), DC).astype(bf16),
        "bq": np.asarray(inputs["L0_attn_bq"])[permg].reshape(1, D).astype(bf16),
        "bk": np.asarray(inputs["L0_attn_bk"])[permg].reshape(1, D).astype(bf16),
        "bv": np.asarray(inputs["L0_attn_bv"]).reshape(1, D).astype(bf16),
        "bo_col": np.ascontiguousarray(
            np.asarray(inputs["L0_attn_bo"], np.float32).reshape(DC, 128).T),
        "wupZ": _z2img(wupT, DC, IC),
        "wgZ": _z2img(wgT, DC, IC),
        "wdZ": _z2img(wdT, IC, DC),
        "ropec": rc, "ropes": rs,
        "ones1": np.ones((1, 128), dtype=bf16),
        "identf": np.eye(128, dtype=np.float32),
        "blockones": np.ascontiguousarray(np.stack(
            [np.r_[np.ones(64), np.zeros(64)],
             np.r_[np.zeros(64), np.ones(64)]], axis=1)).astype(bf16),
    }
    x = np.asarray(inputs["L0_attn_x"], dtype=np.float32)
    return [dict(shared, x=np.ascontiguousarray(x[2 * c:2 * c + 2]))
            for c in range(NCORES)]


def build_program(tap=None, phases=("dq", "attn0", "attn1", "ffn")):
    nc = bass.Bass("TRN2", target_bir_lowering=False)

    xh = nc.dram_tensor("x", [B, S, D], F32, kind="ExternalInput").ap()
    w_h = {k: nc.dram_tensor(k, [128, DC * D], BF16, kind="ExternalInput").ap()
           for k in ("wqT", "wkT", "wvT", "woT")}
    b_h = {k: nc.dram_tensor(k, [1, D], BF16, kind="ExternalInput").ap()
           for k in ("bq", "bk", "bv")}
    boh = nc.dram_tensor("bo_col", [128, DC], F32, kind="ExternalInput").ap()
    wZh = {"up": nc.dram_tensor("wupZ", [128, DC * I3], F32, kind="ExternalInput").ap(),
           "g": nc.dram_tensor("wgZ", [128, DC * I3], F32, kind="ExternalInput").ap(),
           "d": nc.dram_tensor("wdZ", [128, IC * D], F32, kind="ExternalInput").ap()}
    ropech = nc.dram_tensor("ropec", [S, 512], BF16, kind="ExternalInput").ap()
    ropesh = nc.dram_tensor("ropes", [S, 512], BF16, kind="ExternalInput").ap()
    ones1h = nc.dram_tensor("ones1", [1, 128], BF16, kind="ExternalInput").ap()
    idfh = nc.dram_tensor("identf", [128, 128], F32, kind="ExternalInput").ap()
    blkoh = nc.dram_tensor("blockones", [128, 2], BF16, kind="ExternalInput").ap()
    outh = nc.dram_tensor("out", [B, S, D], F32, kind="ExternalOutput").ap()

    x1_scr = nc.dram_tensor("x1_scr", [B * NT, 128, D], F32).ap()
    xq_scr = nc.dram_tensor("xq_scr", [128, DC * B * S], BF16).ap()
    spill = {k: nc.dram_tensor(k + "_scr", [TG, IC, 128, 512], BF16).ap()
             for k in ("up", "gate", "g2", "o")}
    wdq_h = {"up": nc.dram_tensor("wup_dq", [128, DC * I3], BF16).ap(),
             "g": nc.dram_tensor("wg_dq", [128, DC * I3], BF16).ap(),
             "d": nc.dram_tensor("wd_dq", [128, IC * D], BF16).ap()}
    lcol_d = nc.dram_tensor("lcol_d", [2, (H // 2) * SB], BF16).ap()
    lrec_d = nc.dram_tensor("lrec_d", [H, SB], BF16).ap()
    sc_d = nc.dram_tensor("sc_d", [1, 32], F32).ap()
    col_d = nc.dram_tensor("col_d", [1, 128], F32).ap()
    ncol_d = nc.dram_tensor("ncol_d", [2, 128], F32).ap()
    cc_io = [(nc.dram_tensor(f"cc_in{i}", [1, 8], F32).ap(),
              nc.dram_tensor(f"cc_out{i}", [1, 8], F32).ap()) for i in range(4)]

    dbg = None
    if tap is not None:
        shp = {"ffn4": ([4, TG, IC, 128, 512], BF16),
               "x1": ([B * NT, 128, D], F32), "xq": ([128, DC * B * S], BF16),
               "up": ([TG, IC, 128, 512], BF16), "gate": ([TG, IC, 128, 512], BF16),
               "g2": ([TG, IC, 128, 512], BF16), "o": ([TG, IC, 128, 512], BF16),
               "wupdq": ([128, DC * I3], BF16), "wgdq": ([128, DC * I3], BF16),
               "wddq": ([128, IC * D], BF16)}[tap]
        dbg = nc.dram_tensor("dbg", shp[0], shp[1], kind="ExternalOutput").ap()

    P = Plan()
    es = contextlib.ExitStack()
    with es:
        psh = es.enter_context(nc.psum_tensor("ps", [128, 4096], F32))
        psap = psh.ap()

        def bank(i, n=1, rows=128):
            return psap[0:rows, i * 512:(i + n) * 512]

        ARENA_B = 200 * 1024
        sbh = es.enter_context(nc.sbuf_tensor("arena", [128, ARENA_B], U8))
        A = Arena(sbh, ARENA_B)
        sems = {n: es.enter_context(nc.semaphore("s_" + n))
                for n in ("tensor", "vector", "scalar", "gpsimd", "cc") + DMA_SEMS}

        ones1 = A.alloc(128, BF16, rows=1)
        blko = A.alloc(2, BF16)
        idf = A.alloc(128, F32)
        bsb = {k: A.alloc(D, BF16, rows=1) for k in ("bq", "bk", "bv")}
        bo_sb = A.alloc(DC, F32)
        scal = A.alloc(16, F32)
        SCL = dict(negm0=0, sx=1, isx=2, sup=3, isup=4, sg=5, isg=6,
                   sg2=7, isg2=8, so=9, iso=10, gtmp=11)

        def scol(n):
            return scal[:, SCL[n]:SCL[n] + 1]

        stat = A.alloc(8, F32)
        smallf = A.alloc(256, F32)
        amax = A.alloc(16, F32)
        lnj = A.alloc(2048, BF16)
        xf32 = [A.alloc(D, F32) for _ in range(2)]
        x1t = A.alloc(D, F32)
        lntmp = [A.alloc(D, BF16) for _ in range(2)]

        t0 = []
        for src, dst in ((ones1h, ones1), (blkoh, blko), (idfh, idf),
                         (boh, bo_sb), (b_h["bq"], bsb["bq"]),
                         (b_h["bk"], bsb["bk"]), (b_h["bv"], bsb["bv"])):
            t0.append(P.op("sync", lambda e, s=src, d=dst:
                           e.dma_start(out=d, in_=s[:, :]), dma=True))
        tok_const = t0[-1]

        WAR0 = {}

        def ts(eng, out, in0, s1, s2, op0, op1=None, accum=None, deps=()):
            kw = {}
            if accum is not None:
                kw["accum_out"] = accum
            if op1 is not None:
                return P.op(eng, lambda e, o=out, i=in0, a=s1, b=s2, kw=kw:
                            e.tensor_scalar(o, i, a, b, op0, op1, **kw),
                            deps=deps, sig=True)
            return P.op(eng, lambda e, o=out, i=in0, a=s1, kw=kw:
                        e.tensor_scalar(o, i, a, None, op0, **kw),
                        deps=deps, sig=True)

        def tt(eng, out, in0, in1, op, deps=()):
            return P.op(eng, lambda e, o=out, a=in0, b=in1:
                        e.tensor_tensor(o, a, b, op), deps=deps, sig=True)

        def vcopy(out, in0, deps=()):
            return P.op("vector", lambda e, o=out, i=in0: e.tensor_copy(o, i),
                        deps=deps, sig=True)

        def scopy(out, in0, deps=()):
            return P.op("scalar", lambda e, o=out, i=in0: e.copy(o, i),
                        deps=deps, sig=True)

        def ln_stats(xtile, slot, deps):
            c = slot * 4
            su, sq = stat[:, c:c + 1], stat[:, c + 1:c + 2]
            mn, rs = stat[:, c + 2:c + 3], stat[:, c + 3:c + 4]
            w = smallf[:, 16 + 4 * slot:16 + 4 * slot + 3]
            nfd = xtile.shape[-1]
            t1 = P.op("vector", lambda e, x=xtile, o=su: e.reduce_sum(o, x, axis=AX.X),
                      deps=deps, sig=True)
            t2 = P.op("scalar", lambda e, x=xtile, o=sq, n=nfd: e.activation(
                lnj[:, 0:n], x, AF.Square, accum_out=o), deps=deps, sig=True)
            t3 = ts("vector", mn, su, 1.0 / D, None, ALU.mult, deps=[t1])
            t4 = tt("vector", w[:, 0:1], mn, mn, ALU.mult, deps=[t3])
            t5 = ts("vector", w[:, 1:2], sq, 1.0 / D, None, ALU.mult, deps=[t2])
            t6 = tt("vector", w[:, 1:2], w[:, 1:2], w[:, 0:1], ALU.subtract,
                    deps=[t4, t5])
            t7 = ts("vector", w[:, 1:2], w[:, 1:2], EPS, None, ALU.add, deps=[t6])
            t8 = P.op("scalar", lambda e, i=w[:, 1:2], o=w[:, 2:3]: e.activation(
                o, i, AF.Sqrt), deps=[t7], sig=True)
            t9 = P.op("vector", lambda e, i=w[:, 2:3], o=rs: e.reciprocal(o, i),
                      deps=[t8], sig=True)
            return mn, rs, t9, t2

        def collapse2(col_ap, deps, slot):
            ta = P.op("gpsimd", lambda e, c=col_ap: e.dma_start(
                out=col_d[0:1, :], in_=c), deps=list(deps) + [WAR0.get("colD")],
                dma=True)
            tb = P.op("sync", lambda e: e.dma_start(
                out=smallf[0:1, 64:192], in_=col_d[0:1, :]), deps=[ta], dma=True)
            WAR0["colD"] = tb
            tc = P.op("vector", lambda e: e.reduce_max(
                smallf[0:1, 40:41], smallf[0:1, 64:192], axis=AX.X),
                deps=[tb], sig=True)
            td = P.op("gpsimd", lambda e, s=slot: e.dma_start(
                out=sc_d[0:1, s:s + 1], in_=smallf[0:1, 40:41]),
                deps=[tc], dma=True)
            return td, tb

        def bcast_col(dram_ap, col, deps):
            return P.op("sync", lambda e, d=dram_ap, c=col: e.dma_start(
                out=c, in_=d.broadcast_to([128, 1])), deps=deps, dma=True)

        def absrun(src_bf, mxcol, mncol, tmpcol, deps, junk=None):
            nfd = src_bf.shape[-1]
            j = junk if junk is not None else lnj[:, 0:nfd]
            t1 = ts("vector", j, src_bf, 1.0, None, ALU.mult,
                    ALU.max, accum=tmpcol, deps=deps)
            t2 = tt("vector", mxcol, mxcol, tmpcol, ALU.max, deps=[t1])
            t3 = ts("vector", j, src_bf, 1.0, None, ALU.mult,
                    ALU.min, accum=tmpcol, deps=[t2])
            t4 = tt("vector", mncol, mncol, tmpcol, ALU.min, deps=[t3])
            return t4

        A.push()
        zbuf = [A.alloc(I3, F32) for _ in range(2)]
        ubuf0 = A.alloc(I3, F32)
        zqf = A.alloc(I3, F32)
        mxb = A.alloc(96, F32)
        ssb = A.alloc(96, F32)
        sib = A.alloc(96, F32)
        wdqb = [A.alloc(I3, BF16) for _ in range(2)]
        dq_store = []
        last_tr = None
        last_ev = {}
        prev_zq = None
        dq_list = (("up", DC, IC), ("g", DC, IC), ("d", IC, DC)) if "dq" in phases else ()
        for wname, pch, fch in dq_list:
            for c in range(pch):
                par = len(dq_store) % 2
                K, G = fch * 128, fch * 4
                t_ld = P.op("sync", lambda e, s=wZh[wname][:, c * K:(c + 1) * K],
                            d=zbuf[par][:, 0:K]: e.dma_start(out=d, in_=s),
                            deps=[dq_store[-2] if len(dq_store) >= 2 else None],
                            dma=True)
                t_mx = P.op("vector", lambda e,
                            i=zbuf[par][:, 0:K].rearrange("p (g j) -> p g j", j=32),
                            o=mxb[:, 0:G]: e.reduce_max(
                                o, i, axis=AX.X, apply_absolute_value=True),
                            deps=[t_ld], sig=True)
                t_s = ts("vector", ssb[:, 0:G], mxb[:, 0:G], 1.0 / 7.0, 1e-12,
                         ALU.mult, ALU.max, deps=[t_mx, prev_zq])
                t_si = P.op("vector", lambda e, i=ssb[:, 0:G], o=sib[:, 0:G]:
                            e.reciprocal(o, i), deps=[t_s], sig=True)
                t_u = tt("vector",
                         ubuf0[:, 0:K].rearrange("p (g j) -> p g j", j=32),
                         zbuf[par][:, 0:K].rearrange("p (g j) -> p g j", j=32),
                         sib[:, 0:G].unsqueeze(2).broadcast_to([128, G, 32]),
                         ALU.mult, deps=[t_si, last_tr, prev_zq])
                t_q = ts("vector", ubuf0[:, 0:K], ubuf0[:, 0:K], MAGIC, MAGIC,
                         ALU.add, ALU.subtract, deps=[t_u])
                t_zq = tt("vector",
                          zqf[:, 0:K].rearrange("p (g j) -> p g j", j=32),
                          ubuf0[:, 0:K].rearrange("p (g j) -> p g j", j=32),
                          ssb[:, 0:G].unsqueeze(2).broadcast_to([128, G, 32]),
                          ALU.mult, deps=[t_q, last_tr])
                prev_zq = t_zq
                ev_toks = []
                for fb0 in range(0, fch, 4):
                    nsl = min(4, fch - fb0)
                    slot = (fb0 // 4) % 4
                    bk_i = 4 + slot
                    for sl in range(nsl):
                        fb = fb0 + sl
                        dep_tr = [t_zq]
                        if sl == 0:
                            dep_tr.append(last_ev.get(slot))
                        tk_ = P.op("tensor", lambda e,
                                   o=bank(bk_i)[:, sl * 128:(sl + 1) * 128],
                                   i=zqf[:, fb * 128:(fb + 1) * 128]:
                                   e.transpose(o, i, idf),
                                   deps=dep_tr, sig=(sl == nsl - 1))
                    last_tr = tk_
                    t_ev = P.op("scalar", lambda e,
                                s=bank(bk_i)[:, 0:nsl * 128],
                                d=wdqb[par][:, fb0 * 128:(fb0 + nsl) * 128]:
                                e.copy(d, s),
                                deps=[tk_, dq_store[-2] if len(dq_store) >= 2
                                      else None], sig=True)
                    last_ev[slot] = t_ev
                    ev_toks.append(t_ev)
                t_st = P.op("gpsimd", lambda e,
                            d=wdq_h[wname][:, c * K:(c + 1) * K],
                            s=wdqb[par][:, 0:K]: e.dma_start(out=d, in_=s),
                            deps=ev_toks, dma=True)
                dq_store.append(t_st)
        tok_dq = dq_store[-1] if dq_store else tok_const
        A.pop()
        run_attn = [b for b in range(B) if f"attn{b}" in phases]

        A.push()
        qT = A.alloc(DC * S, BF16)
        kT = A.alloc(DC * S, BF16)
        vaug = A.alloc(NT * H * VROW, BF16)
        xqtok = A.alloc(D, BF16)
        xqtr = A.alloc(D, BF16)
        A.push()
        wb3 = [A.alloc(DC * D, BF16) for _ in range(3)]
        xlt = [A.alloc(DC * 128, BF16) for _ in range(2)]
        qkt = [A.alloc(D, BF16) for _ in range(2)]
        rtm = [A.alloc(D, BF16) for _ in range(2)]
        rok = [A.alloc(D, BF16) for _ in range(2)]
        ropc = [A.alloc(512, BF16) for _ in range(2)]
        rops = [A.alloc(512, BF16) for _ in range(2)]
        sqb = A.alloc(1024, BF16)
        proj_mark = A.off
        A.pop()
        A.push()
        wo_sb = A.alloc(DC * D, BF16)
        ptil = [A.alloc(4096, BF16)]
        otp = [A.alloc(8 * SB, BF16) for _ in range(2)]
        lcol = A.alloc((H // 2) * SB, BF16, rows=97)
        lrowf = A.alloc(SB, F32, rows=16)
        lrecf = A.alloc(SB, F32, rows=16)
        lrecb = A.alloc(SB, BF16, rows=16)
        rexp = A.alloc(SB, BF16)
        attnT = A.alloc(DC * SB, BF16)
        attok = A.alloc(4 * D, BF16)
        A.pop()

        vaug_r = vaug.rearrange("p (t h j) -> p t h j", h=H, j=VROW)
        qT_r = qT.rearrange("p (c t) -> p c t", t=S)
        kT_r = kT.rearrange("p (c t) -> p c t", t=S)
        wb3_r = [w.rearrange("p (c e) -> p c e", e=D) for w in wb3]
        wo_r = wo_sb.rearrange("p (c e) -> p c e", e=D)

        t_init = P.op("vector", lambda e: e.memset(vaug_r[:, :, :, 64:65], 1.0),
                      sig=True)
        t_am = P.op("vector", lambda e: e.memset(amax[:, 0:1], -3.0e38), sig=True)
        t_am2 = P.op("vector", lambda e: e.memset(amax[:, 2:3], 3.0e38), sig=True)

        tok_out_stores = []
        tok_phase = tok_dq
        war = {}

        for b in run_attn:
            tw3 = []
            for si, wk_ in enumerate(("wqT", "wkT", "wvT")):
                tw3.append(P.op("gpsimd", lambda e, w=w_h[wk_], d=wb3[si]:
                                e.dma_start(out=d, in_=w[:, :]),
                                deps=[tok_phase], dma=True))
            t_nrm0 = P.op("vector", lambda e: e.memset(amax[:, 6:8], 0.0),
                          deps=war.get("nrmD", []), sig=True)
            tq = tk = tv_ = None
            bank_guard = {}
            for tt_i in range(NT):
                par = tt_i % 2
                xt = xf32[par]
                tl = P.op("sync", lambda e, b=b, t=tt_i, d=xt: e.dma_start(
                    out=d, in_=xh[b, t * 128:(t + 1) * 128, :]),
                    deps=[tok_phase] + war.get(("xf", par), []), dma=True)
                tcv = P.op("sync", lambda e, t=tt_i, d=ropc[par]:
                           e.dma_start(out=d, in_=ropech[t * 128:(t + 1) * 128, :]),
                           deps=[war.get(("rope", par))], dma=True)
                tsv = P.op("sync", lambda e, t=tt_i, d=rops[par]:
                           e.dma_start(out=d, in_=ropesh[t * 128:(t + 1) * 128, :]),
                           deps=[war.get(("rope", par))], dma=True)
                mn, rs, tst, tsq1_ = ln_stats(xt, par, [tl])
                ta = ts("vector", lntmp[par], xt, mn, rs, ALU.subtract, ALU.mult,
                        deps=[tst] + war.get(("lntmp", par), []))
                war[("xf", par)] = [ta, tsq1_]
                ttr = P.op("sync", lambda e, src=lntmp[par], d=xlt[par]: e.dma_start(
                    out=d.rearrange("p (c q) -> p c q", q=128),
                    in_=src.rearrange("p (c q) -> p c q", q=128), transpose=True),
                    deps=[ta] + war.get(("xlt", par), []), dma=True)
                war[("lntmp", par)] = [ttr]
                xlt_r = xlt[par].rearrange("p (c q) -> p c q", q=128)
                xlt_toks = []
                for ki, kind in enumerate(("q", "k", "v")):
                    mm = []
                    for ec in range(2):
                        bk_i = 2 * ki + ec
                        dep0 = [ttr, tw3[ki], bank_guard.get(bk_i)]
                        for dc in range(DC):
                            P.op("tensor", lambda e, o=bank(bk_i),
                                 l=xlt_r[:, dc, :],
                                 r=wb3_r[ki][:, dc, ec * 512:(ec + 1) * 512],
                                 s=(dc == 0): e.matmul(o, l, r, start=s, stop=False),
                                 deps=dep0 if dc == 0 else ())
                        tm = P.op("tensor", lambda e, o=bank(bk_i),
                                  r=bsb["b" + kind[0]][:, ec * 512:(ec + 1) * 512]:
                                  e.matmul(o, ones1, r, start=False, stop=True),
                                  sig=True)
                        mm.append(tm)
                        xlt_toks.append(tm)
                    if kind == "v":
                        for ec in range(2):
                            tv = ts("vector", vaug_r[:, tt_i, ec * 8:(ec + 1) * 8, 0:64],
                                    bank(4 + ec).rearrange("p (h j) -> p h j", j=64),
                                    1.0, None, ALU.mult, deps=[mm[ec], t_init])
                            bank_guard[4 + ec] = tv
                            tv_ = tv
                    else:
                        qk = qkt[par]
                        ev = []
                        for ec in range(2):
                            bk_i = 2 * ki + ec
                            tvv = ts("vector", qk[:, ec * 512:(ec + 1) * 512],
                                     bank(bk_i), 1.0, None, ALU.mult,
                                     deps=[mm[ec]] + war.get(("qkt", par), []))
                            bank_guard[bk_i] = tvv
                            ev.append(tvv)
                        qk_r = qk.rearrange("p (h j) -> p h j", j=64)
                        x1v, x2v = qk_r[:, :, 0:32], qk_r[:, :, 32:64]
                        rkey = "lntmp" if kind == "q" else "rok"
                        ro = lntmp[par] if kind == "q" else rok[par]
                        ro_r = ro.rearrange("p (h j) -> p h j", j=64)
                        o1, o2 = ro_r[:, :, 0:32], ro_r[:, :, 32:64]
                        rt_r = rtm[par].rearrange("p (h j) -> p h j", j=64)
                        u1, u2 = rt_r[:, :, 0:32], rt_r[:, :, 32:64]
                        cv = ropc[par].rearrange("p (h j) -> p h j", j=32)
                        sv = rops[par].rearrange("p (h j) -> p h j", j=32)
                        dr = ev + [tcv, tsv] + war.get(("rtm", par), [])
                        r1 = tt("vector", u1, x1v, cv, ALU.mult, deps=dr)
                        r2 = tt("vector", u2, x2v, sv, ALU.mult, deps=dr)
                        r3 = tt("vector", o1, u1, u2, ALU.subtract,
                                deps=[r1, r2] + war.get((rkey, par), []))
                        r4 = tt("vector", u1, x1v, sv, ALU.mult, deps=[r3])
                        r5 = tt("vector", u2, x2v, cv, ALU.mult, deps=[r3])
                        r6 = tt("vector", o2, u1, u2, ALU.add, deps=[r4, r5])
                        dstT = qT_r if kind == "q" else kT_r
                        ttr2 = P.op("sync", lambda e, src=ro, t=tt_i, d=dstT:
                                    e.dma_start(
                                        out=d[:, :, t * 128:(t + 1) * 128],
                                        in_=src.rearrange("p (c q) -> p c q", q=128),
                                        transpose=True), deps=[r6], dma=True)
                        nacc = amax[:, 8 + 2 * ki + par:9 + 2 * ki + par]
                        tsqn = P.op("scalar", lambda e, i=ro, c=nacc:
                                    e.activation(lnj[:, 0:D], i, AF.Square,
                                                 accum_out=c),
                                    deps=[r6] + war.get(("nacc", ki, par), []),
                                    sig=True)
                        nmg = tt("vector", amax[:, 6 + ki:7 + ki],
                                 amax[:, 6 + ki:7 + ki], nacc, ALU.max,
                                 deps=[tsqn, t_nrm0] + war.get(("nrun", ki), []))
                        war[("nacc", ki, par)] = [nmg]
                        war[("nrun", ki)] = [nmg]
                        war[(rkey, par)] = [ttr2, tsqn]
                        war[("rope", par)] = r6
                        war[("rtm", par)] = [r6]
                        war[("qkt", par)] = [r4, r5]
                        if kind == "q":
                            tq = ttr2
                        else:
                            tk = ttr2
                war[("xlt", par)] = xlt_toks
            for p_ in range(3):
                war[("bankpair", p_)] = [bank_guard.get(2 * p_),
                                         bank_guard.get(2 * p_ + 1)]
            tok_wo = P.op("gpsimd", lambda e: e.dma_start(out=wo_sb, in_=w_h["woT"][:, :]),
                          deps=[tv_], dma=True)

            tnq = P.op("gpsimd", lambda e: e.dma_start(
                out=ncol_d[0:1, :], in_=amax[:, 6:7]),
                deps=war.get(("nrun", 0), []) + [WAR0.get("colD")], dma=True)
            tnk = P.op("gpsimd", lambda e: e.dma_start(
                out=ncol_d[1:2, :], in_=amax[:, 7:8]),
                deps=war.get(("nrun", 1), []), dma=True)
            war["nrmD"] = [tnq, tnk]
            tn4 = P.op("sync", lambda e: e.dma_start(
                out=smallf[0:2, 64:192], in_=ncol_d[:, :]),
                deps=[tnq, tnk, WAR0.get("colD")], dma=True)
            tn5 = P.op("vector", lambda e: e.reduce_max(
                smallf[0:2, 42:43], smallf[0:2, 64:192], axis=AX.X),
                deps=[tn4], sig=True)
            tn7 = P.op("scalar", lambda e: e.activation(
                smallf[0:2, 43:44], smallf[0:2, 42:43], AF.Sqrt),
                deps=[tn5], sig=True)
            tn3 = P.op("gpsimd", lambda e: e.dma_start(
                out=col_d[0:1, 0:2], in_=smallf[0:2, 43:44]),
                deps=[tn7, WAR0.get("colD")], dma=True)
            tn4b = P.op("sync", lambda e: e.dma_start(
                out=smallf[0:1, 48:50], in_=col_d[0:1, 0:2]),
                deps=[tn3], dma=True)
            WAR0["colD"] = tn4b
            tn9 = tt("vector", smallf[0:1, 56:57], smallf[0:1, 48:49],
                     smallf[0:1, 49:50], ALU.mult, deps=[tn4b])
            tn10 = ts("vector", smallf[0:1, 57:58], smallf[0:1, 56:57], -0.125,
                      None, ALU.mult, deps=[tn9])
            tn11 = P.op("gpsimd", lambda e: e.dma_start(
                out=sc_d[0:1, 16:17], in_=smallf[0:1, 57:58]),
                deps=[tn10, war.get("scD16")], dma=True)
            tok_m0 = bcast_col(sc_d[0:1, 16:17], scol("negm0"),
                               [tn11, war.get("negm0")])
            war["scD16"] = tok_m0

            def emit_pipe(sb_i, mid=None):
                buf = sb_i % 2
                evs = {}
                for h in range(H):
                    if mid and h in (2, 5, 8, 11):
                        mid()
                    hp, odd = h // 2, h % 2
                    r0 = 64 * odd
                    tpos = (64, 0) if odd else (0, 0)
                    pairbanks = ((0, 1), (2, 3), (4, 5))
                    expt = {}
                    mmt = {}
                    pv = None
                    for pr in range(NT // 2 + 1):
                        if pr < NT // 2:
                            bks = pairbanks[pr % 3]
                            for j in range(2):
                                tc = 2 * pr + j
                                depq = [tv_, expt.get(pr - 3)] + (
                                    war.get(("bankpair", pr % 3), [])
                                    if pr < 3 else [])
                                mmt[pr] = P.op("tensor", lambda e,
                                    o=bank(bks[j], rows=128),
                                    l=kT_r[r0:r0 + 64, hp, tc * 128:(tc + 1) * 128],
                                    r=qT_r[r0:r0 + 64, hp, sb_i * SB:(sb_i + 1) * SB],
                                    tp=tpos: e.matmul(o, l, r, start=True,
                                                      stop=True, tile_position=tp),
                                    deps=depq if j == 0 else (), sig=(j == 1))
                            expt[pr] = P.op("scalar", lambda e,
                                o=ptil[0][:, (pr % 4) * 1024:(pr % 4 + 1) * 1024],
                                i=bank(bks[0], n=2),
                                : e.activation(o, i, AF.Exp, bias=scol("negm0"),
                                               scale=0.125),
                                deps=[mmt[pr], tok_m0, war.get(("ptil", pr % 4))],
                                sig=True)
                            war[("bankpair", pr % 3)] = [expt[pr]]
                        if pr >= 1:
                            po = pr - 1
                            for j in range(2):
                                tc = 2 * po + j
                                depv = [expt[po]] + (war.get("pv_evac", [])
                                                     if tc == 0 else [])
                                pv = P.op("tensor", lambda e, o=bank(6, rows=65),
                                    l=vaug_r[:, tc, h, 0:65],
                                    r=ptil[0][:, (po % 4) * 1024 + j * 512:
                                              (po % 4) * 1024 + (j + 1) * 512],
                                    s=(tc == 0), p=(tc == NT - 1):
                                    e.matmul(o, l, r, start=s, stop=p),
                                    deps=depv if (j == 0 or tc == 0) else (),
                                    sig=(tc == NT - 1))
                            war[("ptil", po % 4)] = pv
                    tev = vcopy(otp[buf][r0:r0 + 64, hp * SB:(hp + 1) * SB],
                                bank(6, rows=64),
                                deps=[pv, war.get(("otp", buf, hp))])
                    lr = r0 + 32 * buf
                    tl = vcopy(lcol[lr:lr + 1, hp * SB:(hp + 1) * SB],
                               psap[64:65, 6 * 512:7 * 512],
                               deps=[pv] + war.get(("lcol", buf), []))
                    war["pv_evac"] = [tev, tl]
                    war["last_exp"] = expt[NT // 2 - 1]
                    evs[h] = (tev, tl)
                return [(hp, evs[2 * hp][0], evs[2 * hp + 1][0],
                         evs[2 * hp][1], evs[2 * hp + 1][1])
                        for hp in range(H // 2)]

            def emit_lchain(sb_i, pair_toks):
                buf = sb_i % 2
                tg1 = P.op("gpsimd", lambda e: e.dma_start(
                    out=lcol_d[0:1, :], in_=lcol[32 * buf:32 * buf + 1, :]),
                    deps=[t[3] for t in pair_toks] + [war.get("lrowf")], dma=True)
                tg2 = P.op("gpsimd", lambda e: e.dma_start(
                    out=lcol_d[1:2, :], in_=lcol[64 + 32 * buf:65 + 32 * buf, :]),
                    deps=[t[4] for t in pair_toks], dma=True)
                tg3 = P.op("gpsimd", lambda e: e.dma_start(
                    out=lrowf, in_=lcol_d[:, :].rearrange("r (h j) -> (r h) j", j=SB)),
                    deps=[tg1, tg2], dma=True)
                war[("lcol", buf)] = [tg1, tg2]
                tg4 = P.op("vector", lambda e: e.reciprocal(lrecf, lrowf),
                           deps=[tg3], sig=True)
                war["lrowf"] = tg4
                tg5 = ts("vector", lrecb, lrecf, 1.0, None, ALU.mult, deps=[tg4])
                tg6 = P.op("gpsimd", lambda e: e.dma_start(
                    out=lrec_d[:, :], in_=lrecb),
                    deps=[tg5, war.get("lrecd")], dma=True)
                pre0a = P.op("sync", lambda e: e.dma_start(
                    out=rexp[0:64, :],
                    in_=lrec_d[0:1, :].broadcast_to([64, SB])),
                    deps=[tg6, war.get("rexp")], dma=True)
                pre0b = P.op("sync", lambda e: e.dma_start(
                    out=rexp[64:128, :],
                    in_=lrec_d[8:9, :].broadcast_to([64, SB])),
                    deps=[tg6, war.get("rexp")], dma=True)
                return tg6, pre0a, pre0b

            def emit_tail(sb_i, pair_toks, lch):
                buf = sb_i % 2
                tg6, pre0a, pre0b = lch
                div_toks = []
                rexp_guard = None
                for hp, tev1, tev2, _, _ in pair_toks:
                    if hp == 0:
                        tr1a, tr1 = pre0a, pre0b
                    else:
                        tr1a = P.op("sync", lambda e, hp=hp: e.dma_start(
                            out=rexp[0:64, :],
                            in_=lrec_d[hp:hp + 1, :].broadcast_to([64, SB])),
                            deps=[tg6, rexp_guard], dma=True)
                        tr1 = P.op("sync", lambda e, hp=hp: e.dma_start(
                            out=rexp[64:128, :],
                            in_=lrec_d[8 + hp:9 + hp, :].broadcast_to([64, SB])),
                            deps=[tg6, rexp_guard], dma=True)
                    td = tt("gpsimd", otp[buf][:, hp * SB:(hp + 1) * SB],
                            otp[buf][:, hp * SB:(hp + 1) * SB], rexp, ALU.mult,
                            deps=[tr1a, tr1, tev1, tev2])
                    rexp_guard = td
                    div_toks.append(td)
                war["rexp"] = rexp_guard
                war["lrecd"] = div_toks[-1]
                ev2 = []
                bank6_guard = war.get("bank7")
                for fc in range(DC):
                    for pp in range(DC):
                        tmo = P.op("tensor", lambda e, o=bank(7),
                                   l=wo_r[:, pp, fc * 128:(fc + 1) * 128],
                                   r=otp[buf][:, pp * SB:(pp + 1) * SB],
                                   s=(pp == 0), p=(pp == DC - 1):
                                   e.matmul(o, l, r, start=s, stop=p),
                                   deps=[div_toks[-1], tok_wo, bank6_guard]
                                   if pp == 0 else (), sig=(pp == DC - 1))
                    tev = ts("vector", attnT[:, fc * SB:(fc + 1) * SB], bank(7),
                             bo_sb[:, fc:fc + 1], None, ALU.add,
                             deps=[tmo, war.get("attnT")])
                    bank6_guard = tev
                    ev2.append(tev)
                war["bank7"] = bank6_guard
                for hp in range(DC):
                    war[("otp", buf, hp)] = ev2[-1]
                ttr3 = None
                for fc in range(DC):
                    ttr3 = P.op("sync", lambda e, fc=fc: e.dma_start(
                        out=attok.rearrange("p (w f q) -> p w f q", f=DC, q=128
                                            )[:, :, fc, :],
                        in_=attnT[:, fc * SB:(fc + 1) * SB],
                        transpose=True),
                        deps=[ev2[fc]] + war.get("attok", []), dma=True)
                war["attnT"] = ttr3
                return ttr3

            def emit_tail_ln(sb_i, ttr3):
                for tw in range(4):
                    gt = sb_i * 4 + tw
                    par = gt % 2
                    atile = attok[:, tw * D:(tw + 1) * D]
                    mn, rs, tst, tsq_ = ln_stats(atile, par, [ttr3])
                    t_ln2 = ts("vector", lntmp[par], atile, mn, rs,
                               ALU.subtract, ALU.mult,
                               deps=[tst] + war.get(("lntmp", par), []))
                    t_xl = P.op("sync", lambda e, d=xf32[par], b=b, g=gt:
                                e.dma_start(out=d, in_=xh[b, g * 128:(g + 1) * 128, :]),
                                deps=war.get(("xf", par), []), dma=True)
                    t_x1 = tt("vector", x1t, xf32[par], lntmp[par], ALU.add,
                              deps=[t_xl, t_ln2] + war.get("x1t", []))
                    war[("xf", par)] = [t_x1]
                    t_x1s = P.op("gpsimd", lambda e, b=b, g=gt: e.dma_start(
                        out=x1_scr[b * NT + g], in_=x1t), deps=[t_x1], dma=True)
                    mn2, rs2, tst2, tsq2_ = ln_stats(x1t, par, [t_x1])
                    t_xq = ts("vector", xqtok, x1t, mn2, rs2, ALU.subtract,
                              ALU.mult, deps=[tst2, war.get("xqtok")])
                    war["x1t"] = [t_x1s, tsq2_]
                    t_ab = absrun(xqtok, amax[:, 0:1], amax[:, 2:3],
                                  amax[:, 4:5], [t_xq, t_am, t_am2])
                    t_tr4 = P.op("sync", lambda e: e.dma_start(
                        out=xqtr.rearrange("p (c q) -> p c q", q=128),
                        in_=xqtok.rearrange("p (c q) -> p c q", q=128),
                        transpose=True), deps=[t_xq, war.get("xqtr")], dma=True)
                    war["xqtok"] = t_tr4
                    t_xqs = P.op("gpsimd", lambda e, b=b, g=gt: e.dma_start(
                        out=xq_scr.rearrange("p (c t) -> p c t", t=B * S)[
                            :, :, (b * S + g * 128):(b * S + (g + 1) * 128)],
                        in_=xqtr.rearrange("p (c q) -> p c q", q=128)),
                        deps=[t_tr4], dma=True)
                    war["xqtr"] = t_xqs
                    war[("lntmp", par)] = [t_x1]
                    war["attok"] = [tsq_, t_ln2]
                    lnres["tok"] = t_xqs
                    yield

            pend = []
            lnq = []
            lnres = {"tok": None}

            def _lnstep(n):
                while lnq and n > 0:
                    try:
                        next(lnq[0])
                        n -= 1
                    except StopIteration:
                        lnq.pop(0)

            def _mid():
                _lnstep(1)

            for sb_i in range(NSB):
                pt = emit_pipe(sb_i, _mid)
                if pend:
                    sbp, ptp, lchp = pend.pop(0)
                    t3_ = emit_tail(sbp, ptp, lchp)
                    lnq.append(emit_tail_ln(sbp, t3_))
                pend.append((sb_i, pt, emit_lchain(sb_i, pt)))
            sbp, ptp, lchp = pend.pop(0)
            _lnstep(99)
            t3_ = emit_tail(sbp, ptp, lchp)
            for _ in emit_tail_ln(sbp, t3_):
                pass
            tok_phase = lnres["tok"]
            war["negm0"] = war.get("last_exp")
        tok_attn_done = tok_phase
        A.pop()

        A.push()
        wff = A.alloc(DC * I3, BF16)
        mxc = {k: A.alloc(4, F32) for k in ("up", "gate", "g2", "o")}
        acc_t = A.alloc(2, F32)
        A.push()
        wff2 = A.alloc(DC * I3, BF16)
        xqb = [A.alloc(DC * 512, BF16) for _ in range(2)]
        xqq = [A.alloc(DC * 512, BF16) for _ in range(2)]
        qtmp = A.alloc(DC * 512, F32)
        spb = [A.alloc(2048, BF16) for _ in range(2)]
        A.pop()
        A.push()
        gb = [A.alloc(4096, BF16) for _ in range(2)]
        g2b = [A.alloc(4096, BF16) for _ in range(2)]
        ub = [A.alloc(4096, BF16) for _ in range(2)]
        sgb = [A.alloc(4096, BF16) for _ in range(2)]
        ftmp = [A.alloc(4096, F32) for _ in range(2)]
        A.pop()
        A.push()
        ob = A.alloc(IC * 512, BF16)
        oq = A.alloc(IC * 512, BF16)
        dnT = A.alloc(DC * 512, BF16)
        dntok = A.alloc(4 * D, BF16)
        ftmp4 = [A.alloc(2048, F32) for _ in range(2)]
        A.pop()

        def ar(idx, cols, names, deps):
            toks = []
            guard = list(deps) + [WAR0.get("ar")]
            for i, (mxcol, mncol) in enumerate(cols):
                tn = ts("vector", smallf[:, 60:61], mncol, -1.0, None, ALU.mult,
                        deps=guard)
                tm = tt("vector", smallf[:, 61:62], mxcol, smallf[:, 60:61],
                        ALU.max, deps=[tn])
                tcx, _ = collapse2(smallf[:, 61:62], [tm], i)
                toks.append(P.op("gpsimd", lambda e, i=i, idx=idx: e.dma_start(
                    out=cc_io[idx][0][0:1, i:i + 1], in_=sc_d[0:1, i:i + 1]),
                    deps=[tcx], dma=True))
                guard = [tcx]
            tcc_ = P.op("gpsimd", lambda e, idx=idx: e.collective_compute(
                "AllReduce", ALU.max, replica_groups=[list(range(NCORES))],
                ins=[cc_io[idx][0].opt()], outs=[cc_io[idx][1].opt()]),
                deps=toks, cc=True)
            last = None
            for i, (sn, isn) in enumerate(names):
                tbx = bcast_col(cc_io[idx][1][0:1, i:i + 1], scol("gtmp"),
                                [tcc_, last])
                tsa = ts("vector", scol(sn), scol("gtmp"), 1.0 / QS, 1e-12,
                         ALU.mult, ALU.max, deps=[tbx])
                last = P.op("vector", lambda e, sn=sn, isn=isn: e.reciprocal(
                    scol(isn), scol(sn)), deps=[tsa], sig=True)
            WAR0["ar"] = last
            return last

        def _ffn_body():
            tok_sx = ar(0, [(amax[:, 0:1], amax[:, 2:3])], [("sx", "isx")],
                        [tok_attn_done])

            t_wu = P.op("sync", lambda e: e.dma_start(out=wff, in_=wdq_h["up"][:, :]),
                        deps=[tok_dq, tok_attn_done], dma=True)
            t_wg = P.op("sync", lambda e: e.dma_start(out=wff2, in_=wdq_h["g"][:, :]),
                        deps=[tok_dq, tok_attn_done], dma=True)
            for k in ("up", "gate", "g2", "o"):
                P.op("vector", lambda e, k=k: e.memset(mxc[k][:, 0:1], -3.0e38), sig=True)
                P.op("vector", lambda e, k=k: e.memset(mxc[k][:, 1:2], 3.0e38), sig=True)
            wff_r = wff.rearrange("p (c i) -> p c i", i=I3)
            wff2_r = wff2.rearrange("p (c i) -> p c i", i=I3)
            xq_scr_r = xq_scr.rearrange("p (c t) -> p c t", t=B * S)

            last_sp = {}
            bank_g = {}
            last_tg = [None, None]
            for tg_i in range(TG):
                par = tg_i % 2
                tld = P.op("sync", lambda e, t=tg_i, d=xqb[par]: e.dma_start(
                    out=d.rearrange("p (c q) -> p c q", q=512),
                    in_=xq_scr_r[:, :, t * 512:(t + 1) * 512]),
                    deps=[tok_sx, last_tg[par]], dma=True)
                tq1 = ts("vector", qtmp, xqb[par], scol("isx"), MAGIC, ALU.mult,
                         ALU.add, deps=[tld, tok_sx, war.get("qtmp")])
                tq2 = ts("vector", xqq[par], qtmp, MAGIC, scol("sx"), ALU.subtract,
                         ALU.mult, deps=[tq1, tok_sx])
                war["qtmp"] = tq2
                last_tg[par] = tq2
                xqr = xqq[par].rearrange("p (c q) -> p c q", q=512)
                for wi, (wr, scr, mx) in enumerate(
                        ((wff_r, spill["up"], mxc["up"]),
                         (wff2_r, spill["gate"], mxc["gate"]))):
                    for icg in range(IC // 4):
                        evg = []
                        sp_par = (tg_i * 2 + icg) % 2
                        for s4 in range(4):
                            ic = icg * 4 + s4
                            bk_i = s4 % 2 + 2 * wi
                            for dc in range(DC):
                                tmf = P.op("tensor", lambda e, o=bank(bk_i),
                                           l=wr[:, dc, ic * 128:(ic + 1) * 128],
                                           r=xqr[:, dc, :], s=(dc == 0),
                                           p=(dc == DC - 1):
                                           e.matmul(o, l, r, start=s, stop=p),
                                           deps=[tq2, t_wu, t_wg, bank_g.get(bk_i)]
                                           if dc == 0 else (), sig=(dc == DC - 1))
                            tev = P.op("scalar", lambda e,
                                       o=spb[sp_par][:, s4 * 512:(s4 + 1) * 512],
                                       i=bank(bk_i): e.copy(o, i),
                                       deps=[tmf, last_sp.get(sp_par)], sig=True)
                            bank_g[bk_i] = tev
                            evg.append(tev)
                            ta = absrun(spb[sp_par][:, s4 * 512:(s4 + 1) * 512],
                                        mx[:, 0:1], mx[:, 1:2], acc_t[:, 0:1], [tev])
                            evg.append(ta)
                        tst_ = P.op("gpsimd", lambda e, t=tg_i, icg=icg, scr=scr,
                                    sp=sp_par: e.dma_start(
                                        out=scr[t, icg * 4:(icg + 1) * 4].transpose([1, 0, 2]),
                                        in_=spb[sp].rearrange("p (a q) -> p a q", q=512)),
                                    deps=evg, dma=True)
                        last_sp[sp_par] = tst_
            tok_ar2 = ar(1, [(mxc["up"][:, 0:1], mxc["up"][:, 1:2]),
                             (mxc["gate"][:, 0:1], mxc["gate"][:, 1:2])],
                         [("sup", "isup"), ("sg", "isg")],
                         [last_sp[0], last_sp[1]])

            prevs = {}
            p2_last = None
            for gidx in range(TG * (IC // 8)):
                tg_i, icg = divmod(gidx, IC // 8)
                par = gidx % 2
                tld = P.op("sync", lambda e, t=tg_i, icg=icg, d=gb[par]: e.dma_start(
                    out=d.rearrange("p (a q) -> p a q", q=512),
                    in_=spill["gate"][t, icg * 8:(icg + 1) * 8].transpose([1, 0, 2])),
                    deps=[tok_ar2, prevs.get(("st", par))], dma=True)
                m1 = ts("vector", ftmp[par], gb[par], scol("isg"), MAGIC, ALU.mult,
                        ALU.add, deps=[tld, prevs.get(("ft", par))])
                m2 = ts("vector", gb[par], ftmp[par], MAGIC, scol("sg"), ALU.subtract,
                        ALU.mult, deps=[m1, tok_ar2])
                m3 = P.op("scalar", lambda e, i=gb[par], o=sgb[par]: e.activation(
                    o, i, AF.Sigmoid), deps=[m2, prevs.get(("sg", par))], sig=True)
                m4 = ts("vector", ftmp[par], sgb[par], 65536.0, MAGIC, ALU.mult,
                        ALU.add, deps=[m3, m2])
                m5 = ts("vector", sgb[par], ftmp[par], MAGIC, 65535.0, ALU.subtract,
                        ALU.min, deps=[m4])
                prevs[("ft", par)] = m5
                m6 = tt("vector", g2b[par], gb[par], sgb[par], ALU.mult,
                        deps=[m5, prevs.get(("g2", par))])
                prevs[("sg", par)] = m6
                m7 = absrun(g2b[par], mxc["g2"][:, 0:1], mxc["g2"][:, 1:2],
                            acc_t[:, 0:1], [m6], junk=gb[par])
                tst_ = P.op("gpsimd", lambda e, t=tg_i, icg=icg, par=par: e.dma_start(
                    out=spill["g2"][t, icg * 8:(icg + 1) * 8].transpose([1, 0, 2]),
                    in_=g2b[par].rearrange("p (a q) -> p a q", q=512)),
                    deps=[m7, m6], dma=True)
                prevs[("g2", par)] = tst_
                prevs[("st", par)] = tst_
                p2_last = tst_
            tok_ar3 = ar(2, [(mxc["g2"][:, 0:1], mxc["g2"][:, 1:2])],
                         [("sg2", "isg2")],
                         [prevs.get(("st", 0)), prevs.get(("st", 1))])

            prevs = {}
            for gidx in range(TG * (IC // 8)):
                tg_i, icg = divmod(gidx, IC // 8)
                par = gidx % 2
                tld1 = P.op("sync", lambda e, t=tg_i, icg=icg, d=gb[par]: e.dma_start(
                    out=d.rearrange("p (a q) -> p a q", q=512),
                    in_=spill["g2"][t, icg * 8:(icg + 1) * 8].transpose([1, 0, 2])),
                    deps=[tok_ar3, prevs.get(("st", par))], dma=True)
                tld2 = P.op("sync", lambda e, t=tg_i, icg=icg, d=ub[par]: e.dma_start(
                    out=d.rearrange("p (a q) -> p a q", q=512),
                    in_=spill["up"][t, icg * 8:(icg + 1) * 8].transpose([1, 0, 2])),
                    deps=[tok_ar3, prevs.get(("st", par))], dma=True)
                m1 = ts("vector", ftmp[par], gb[par], scol("isg2"), MAGIC, ALU.mult,
                        ALU.add, deps=[tld1, prevs.get(("ft", par))])
                m2 = ts("vector", gb[par], ftmp[par], MAGIC, scol("sg2"), ALU.subtract,
                        ALU.mult, deps=[m1, tok_ar3])
                m3 = ts("vector", ftmp[par], ub[par], scol("isup"), MAGIC, ALU.mult,
                        ALU.add, deps=[tld2, m2])
                m4 = ts("vector", ub[par], ftmp[par], MAGIC, scol("sup"), ALU.subtract,
                        ALU.mult, deps=[m3, tok_ar2])
                prevs[("ft", par)] = m4
                m5 = tt("vector", g2b[par], gb[par], ub[par], ALU.mult,
                        deps=[m4, m2, prevs.get(("g2", par))])
                m6 = absrun(g2b[par], mxc["o"][:, 0:1], mxc["o"][:, 1:2],
                            acc_t[:, 0:1], [m5], junk=gb[par])
                tst_ = P.op("gpsimd", lambda e, t=tg_i, icg=icg, par=par: e.dma_start(
                    out=spill["o"][t, icg * 8:(icg + 1) * 8].transpose([1, 0, 2]),
                    in_=g2b[par].rearrange("p (a q) -> p a q", q=512)),
                    deps=[m6, m5], dma=True)
                prevs[("g2", par)] = tst_
                prevs[("st", par)] = tst_
            tok_ar4 = ar(3, [(mxc["o"][:, 0:1], mxc["o"][:, 1:2])], [("so", "iso")],
                         [prevs.get(("st", 0)), prevs.get(("st", 1))])

            t_wd = P.op("sync", lambda e: e.dma_start(out=wff, in_=wdq_h["d"][:, :]),
                        deps=[tok_ar2], dma=True)
            wd_r = wff.rearrange("p (c i) -> p c i", i=D)
            war2 = {}
            for tg_i in range(TG):
                tld = P.op("sync", lambda e, t=tg_i: e.dma_start(
                    out=ob.rearrange("p (a q) -> p a q", q=512),
                    in_=spill["o"][t].transpose([1, 0, 2])),
                    deps=[tok_ar4, war2.get("ob")], dma=True)
                q_toks = []
                qprev = war2.get("oq")
                fprev = None
                for icg in range(IC // 4):
                    fpar = icg % 2
                    m1 = ts("vector", ftmp4[fpar], ob[:, icg * 2048:(icg + 1) * 2048],
                            scol("iso"), MAGIC, ALU.mult, ALU.add,
                            deps=[tld, fprev])
                    m2 = ts("vector", oq[:, icg * 2048:(icg + 1) * 2048], ftmp4[fpar],
                            MAGIC, scol("so"), ALU.subtract, ALU.mult,
                            deps=[m1, qprev, tok_ar4])
                    fprev = m2
                    q_toks.append(m2)
                war2["ob"] = q_toks[-1]
                oqr = oq.rearrange("p (c q) -> p c q", q=512)
                ev3 = []
                bank_pg = {}
                for h2c in range(DC):
                    bk_i = h2c % 4
                    for qc in range(IC):
                        tmf = P.op("tensor", lambda e, o=bank(bk_i),
                                   l=wd_r[:, qc, h2c * 128:(h2c + 1) * 128],
                                   r=oqr[:, qc, :], s=(qc == 0), p=(qc == IC - 1):
                                   e.matmul(o, l, r, start=s, stop=p),
                                   deps=[q_toks[qc // 4], t_wd, bank_pg.get(bk_i),
                                         war2.get(("bank", bk_i))]
                                   if qc % 4 == 0 else (), sig=(qc == IC - 1))
                    tev = ts("vector", dnT[:, h2c * 512:(h2c + 1) * 512], bank(bk_i),
                             1.0 / 65536.0, None, ALU.mult,
                             deps=[tmf, war2.get("dnT")])
                    bank_pg[bk_i] = tev
                    war2[("bank", bk_i)] = tev
                    ev3.append(tev)
                war2["oq"] = ev3[-1]
                ttr5 = None
                for fc in range(DC):
                    ttr5 = P.op("sync", lambda e, fc=fc: e.dma_start(
                        out=dntok.rearrange("p (w f q) -> p w f q", f=DC, q=128
                                            )[:, :, fc, :],
                        in_=dnT[:, fc * SB:(fc + 1) * SB],
                        transpose=True), deps=[ev3[fc]] + [war2.get("dntok")], dma=True)
                war2["dnT"] = ttr5
                last_sq = None
                for tw in range(4):
                    gt = tg_i * 4 + tw
                    b_i, tt_i = gt // NT, gt % NT
                    par = gt % 2
                    atile = dntok[:, tw * D:(tw + 1) * D]
                    mn, rs, tst_, tsq_ = ln_stats(atile, par, [ttr5])
                    t_ln4 = ts("vector", lntmp[par], atile, mn, rs, ALU.subtract,
                               ALU.mult, deps=[tst_] + war.get(("lntmp", par), []))
                    t_xl = P.op("sync", lambda e, d=xf32[par], g=gt: e.dma_start(
                        out=d, in_=x1_scr[g]), deps=war.get(("xf", par), []), dma=True)
                    t_o = tt("vector", x1t, xf32[par], lntmp[par], ALU.add,
                             deps=[t_xl, t_ln4] + war.get("x1t", []))
                    war[("xf", par)] = [t_o]
                    t_os = P.op("gpsimd", lambda e, b=b_i, t=tt_i: e.dma_start(
                        out=outh[b, t * 128:(t + 1) * 128, :], in_=x1t),
                        deps=[t_o], dma=True)
                    war["x1t"] = [t_os]
                    war[("lntmp", par)] = [t_o]
                    last_sq = tsq_
                    tok_out_stores.append(t_os)
                war2["dntok"] = last_sq
            A.pop()

        if "ffn" in phases:
            _ffn_body()

        if not tok_out_stores:
            tok_out_stores.append(tok_attn_done)
        if tap == "ffn4":
            t_dbg = None
            for i, k in enumerate(("up", "gate", "g2", "o")):
                t_dbg = P.op("gpsimd", lambda e, s=spill[k], i=i: e.dma_start(
                    out=dbg[i], in_=s), deps=[tok_out_stores[-1]], dma=True)
            tok_out_stores.append(t_dbg)
        elif tap is not None:
            srcs = {"x1": x1_scr, "xq": xq_scr, "up": spill["up"],
                    "gate": spill["gate"], "g2": spill["g2"], "o": spill["o"],
                    "wupdq": wdq_h["up"], "wgdq": wdq_h["g"], "wddq": wdq_h["d"]}
            t_dbg = P.op("gpsimd", lambda e, s=srcs[tap]: e.dma_start(
                out=dbg, in_=s), deps=[tok_out_stores[-1]], dma=True)
            tok_out_stores.append(t_dbg)

        P.op("gpsimd", lambda e: e.engine_nop(), deps=[tok_out_stores[-1]])

        block = es.enter_context(nc.Block())

        @block.sync
        def _(e):
            P.replay("sync", e, sems)

        @block.gpsimd
        def _(e):
            P.replay("gpsimd", e, sems)

        @block.tensor
        def _(e):
            P.replay("tensor", e, sems)

        @block.vector
        def _(e):
            P.replay("vector", e, sems)

        @block.scalar
        def _(e):
            P.replay("scalar", e, sems)

    build_program.last_plan = P
    return nc


def kernel(**inputs):
    in_maps = _stage(inputs)
    nc = build_program()
    res = run_bass_kernel_spmd(nc, in_maps, core_ids=list(range(NCORES)))
    return np.concatenate(
        [np.asarray(r["out"]) for r in res.results], axis=0).astype(np.float32)


# version 18 (final)
# speedup vs baseline: 1.1862x; 1.0423x over previous
import contextlib
import numpy as np
import ml_dtypes
import concourse.bass as bass
import concourse.mybir as mybir
from concourse.bass_utils import run_bass_kernel_spmd

F32 = mybir.dt.float32
BF16 = mybir.dt.bfloat16
U8 = mybir.dt.uint8
AF = mybir.ActivationFunctionType
ALU = mybir.AluOpType
AX = mybir.AxisListType
bf16 = ml_dtypes.bfloat16

MAGIC = 1.5 * 2.0 ** 23
NCORES = 8
B, S, D, H, HD, I3 = 2, 2048, 1024, 16, 64, 3072
NT, DC, IC = S // 128, D // 128, I3 // 128
SB, NSB, TG = 512, S // 512, B * S // 512
VROW = 66
EPS = 1e-5
QS = 32767.0
ENGS = ("sync", "gpsimd", "tensor", "vector", "scalar")
NDMASEM = 8
DMA_QUEUES = ("sync", "gpsimd", "scalar_dma")
DMA_SEMS = tuple(f"{q}{i}" for q in DMA_QUEUES for i in range(NDMASEM))


class Plan:
    def __init__(self):
        self.steps = {e: [] for e in ENGS}
        self.cnt = {}

    def op(self, eng, emit, deps=(), sig=False, dma=False, cc=False):
        waits = {}
        for d in deps:
            if d is None:
                continue
            sn, v = d
            if waits.get(sn, 0) < v:
                waits[sn] = v
        sem, inc, tok = None, 1, None
        if cc:
            sem = "cc"
        elif dma:
            q = "scalar_dma" if eng == "scalar" else eng
            i = self.cnt.get("_q_" + q, 0)
            self.cnt["_q_" + q] = i + 1
            sem, inc = f"{q}{i % NDMASEM}", 16
            prev = self.cnt.get(sem, 0)
            if prev and waits.get(sem, 0) < prev:
                waits[sem] = prev
        elif sig:
            sem = eng
        if sem is not None:
            self.cnt[sem] = self.cnt.get(sem, 0) + inc
            tok = (sem, self.cnt[sem])
        self.steps[eng].append((waits, emit, sem, inc))
        return tok

    def replay(self, eng_name, eng, sems):
        done = {}
        for waits, emit, sem, inc in self.steps[eng_name]:
            for sn in sorted(waits):
                v = waits[sn]
                if (sn == eng_name == "tensor"):
                    continue
                if done.get(sn, 0) >= v:
                    continue
                eng.wait_ge(sems[sn], v)
                done[sn] = v
            ins = emit(eng)
            if sem is not None:
                ins.then_inc(sems[sem], inc)


class Arena:
    def __init__(self, handle, nbytes):
        self.h, self.nbytes, self.off, self.marks = handle, nbytes, 0, []

    def push(self):
        self.marks.append(self.off)

    def pop(self):
        self.off = self.marks.pop()

    def alloc(self, cols, dtype, rows=128):
        nb = cols * mybir.dt.size(dtype)
        off = self.off
        self.off += (nb + 63) & ~63
        assert self.off <= self.nbytes, f"SBUF overflow {self.off}>{self.nbytes}"
        return self.h.ap()[0:rows, off:off + nb].bitcast(dtype)


def _ropetables():
    inv = 1.0 / (10000.0 ** (np.arange(0, HD, 2, dtype=np.float64) / HD))
    fr = np.outer(np.arange(S, dtype=np.float64), inv)
    return (np.tile(np.cos(fr), (1, H)).astype(bf16),
            np.tile(np.sin(fr), (1, H)).astype(bf16))


def _img(w, pchunks):
    r, x = w.shape
    assert r == pchunks * 128
    return np.ascontiguousarray(
        w.reshape(pchunks, 128, x).transpose(1, 0, 2).reshape(128, pchunks * x))


def _z2img(wt, pchunks, fchunks):
    z = wt.reshape(pchunks, 128, fchunks, 128).transpose(0, 3, 2, 1)
    return np.ascontiguousarray(
        z.reshape(pchunks, 128, fchunks * 128).transpose(1, 0, 2).reshape(
            128, pchunks * fchunks * 128))


def _stage(inputs):
    perm = np.concatenate([np.arange(0, HD, 2), np.arange(1, HD, 2)])
    permg = (np.arange(H)[:, None] * HD + perm[None, :]).reshape(-1)
    wq = np.asarray(inputs["L0_attn_Wq"])[permg, :]
    wk = np.asarray(inputs["L0_attn_Wk"])[permg, :]
    wv = np.asarray(inputs["L0_attn_Wv"])
    wo = np.asarray(inputs["L0_attn_Wo"])
    wupT = np.ascontiguousarray(np.asarray(inputs["L0_ffn_w_up"], np.float32).T)
    wgT = np.ascontiguousarray(np.asarray(inputs["L0_ffn_w_gate"], np.float32).T)
    wdT = np.ascontiguousarray(np.asarray(inputs["L0_ffn_w_down"], np.float32).T)
    rc, rs = _ropetables()
    shared = {
        "wqT": _img(np.ascontiguousarray(wq.T), DC).astype(bf16),
        "wkT": _img(np.ascontiguousarray(wk.T), DC).astype(bf16),
        "wvT": _img(np.ascontiguousarray(wv.T), DC).astype(bf16),
        "woT": _img(np.ascontiguousarray(wo.T), DC).astype(bf16),
        "bq": np.asarray(inputs["L0_attn_bq"])[permg].reshape(1, D).astype(bf16),
        "bk": np.asarray(inputs["L0_attn_bk"])[permg].reshape(1, D).astype(bf16),
        "bv": np.asarray(inputs["L0_attn_bv"]).reshape(1, D).astype(bf16),
        "bo_col": np.ascontiguousarray(
            np.asarray(inputs["L0_attn_bo"], np.float32).reshape(DC, 128).T),
        "wupZ": _z2img(wupT, DC, IC),
        "wgZ": _z2img(wgT, DC, IC),
        "wdZ": _z2img(wdT, IC, DC),
        "ropec": rc, "ropes": rs,
        "ones1": np.ones((1, 128), dtype=bf16),
        "identf": np.eye(128, dtype=np.float32),
        "blockones": np.ascontiguousarray(np.stack(
            [np.r_[np.ones(64), np.zeros(64)],
             np.r_[np.zeros(64), np.ones(64)]], axis=1)).astype(bf16),
    }
    x = np.asarray(inputs["L0_attn_x"], dtype=np.float32)
    return [dict(shared, x=np.ascontiguousarray(x[2 * c:2 * c + 2]))
            for c in range(NCORES)]


def build_program(tap=None, phases=("dq", "attn0", "attn1", "ffn")):
    nc = bass.Bass("TRN2", target_bir_lowering=False)

    xh = nc.dram_tensor("x", [B, S, D], F32, kind="ExternalInput").ap()
    w_h = {k: nc.dram_tensor(k, [128, DC * D], BF16, kind="ExternalInput").ap()
           for k in ("wqT", "wkT", "wvT", "woT")}
    b_h = {k: nc.dram_tensor(k, [1, D], BF16, kind="ExternalInput").ap()
           for k in ("bq", "bk", "bv")}
    boh = nc.dram_tensor("bo_col", [128, DC], F32, kind="ExternalInput").ap()
    wZh = {"up": nc.dram_tensor("wupZ", [128, DC * I3], F32, kind="ExternalInput").ap(),
           "g": nc.dram_tensor("wgZ", [128, DC * I3], F32, kind="ExternalInput").ap(),
           "d": nc.dram_tensor("wdZ", [128, IC * D], F32, kind="ExternalInput").ap()}
    ropech = nc.dram_tensor("ropec", [S, 512], BF16, kind="ExternalInput").ap()
    ropesh = nc.dram_tensor("ropes", [S, 512], BF16, kind="ExternalInput").ap()
    ones1h = nc.dram_tensor("ones1", [1, 128], BF16, kind="ExternalInput").ap()
    idfh = nc.dram_tensor("identf", [128, 128], F32, kind="ExternalInput").ap()
    blkoh = nc.dram_tensor("blockones", [128, 2], BF16, kind="ExternalInput").ap()
    outh = nc.dram_tensor("out", [B, S, D], F32, kind="ExternalOutput").ap()

    x1_scr = nc.dram_tensor("x1_scr", [B * NT, 128, D], F32).ap()
    xq_scr = nc.dram_tensor("xq_scr", [128, DC * B * S], BF16).ap()
    spill = {k: nc.dram_tensor(k + "_scr", [TG, IC, 128, 512], BF16).ap()
             for k in ("up", "gate", "g2", "o")}
    wdq_h = {"up": nc.dram_tensor("wup_dq", [128, DC * I3], BF16).ap(),
             "g": nc.dram_tensor("wg_dq", [128, DC * I3], BF16).ap(),
             "d": nc.dram_tensor("wd_dq", [128, IC * D], BF16).ap()}
    lcol_d = nc.dram_tensor("lcol_d", [2, (H // 2) * SB], BF16).ap()
    lrec_d = nc.dram_tensor("lrec_d", [H, SB], BF16).ap()
    sc_d = nc.dram_tensor("sc_d", [1, 32], F32).ap()
    col_d = nc.dram_tensor("col_d", [1, 128], F32).ap()
    ncol_d = nc.dram_tensor("ncol_d", [2, 128], F32).ap()
    cc_io = [(nc.dram_tensor(f"cc_in{i}", [1, 8], F32).ap(),
              nc.dram_tensor(f"cc_out{i}", [1, 8], F32).ap()) for i in range(4)]

    dbg = None
    if tap is not None:
        shp = {"ffn4": ([4, TG, IC, 128, 512], BF16),
               "x1": ([B * NT, 128, D], F32), "xq": ([128, DC * B * S], BF16),
               "up": ([TG, IC, 128, 512], BF16), "gate": ([TG, IC, 128, 512], BF16),
               "g2": ([TG, IC, 128, 512], BF16), "o": ([TG, IC, 128, 512], BF16),
               "wupdq": ([128, DC * I3], BF16), "wgdq": ([128, DC * I3], BF16),
               "wddq": ([128, IC * D], BF16)}[tap]
        dbg = nc.dram_tensor("dbg", shp[0], shp[1], kind="ExternalOutput").ap()

    P = Plan()
    es = contextlib.ExitStack()
    with es:
        psh = es.enter_context(nc.psum_tensor("ps", [128, 4096], F32))
        psap = psh.ap()

        def bank(i, n=1, rows=128):
            return psap[0:rows, i * 512:(i + n) * 512]

        ARENA_B = 200 * 1024
        sbh = es.enter_context(nc.sbuf_tensor("arena", [128, ARENA_B], U8))
        A = Arena(sbh, ARENA_B)
        sems = {n: es.enter_context(nc.semaphore("s_" + n))
                for n in ("tensor", "vector", "scalar", "gpsimd", "cc") + DMA_SEMS}

        ones1 = A.alloc(128, BF16, rows=1)
        blko = A.alloc(2, BF16)
        idf = A.alloc(128, F32)
        bsb = {k: A.alloc(D, BF16, rows=1) for k in ("bq", "bk", "bv")}
        bo_sb = A.alloc(DC, F32)
        scal = A.alloc(16, F32)
        SCL = dict(negm0=0, sx=1, isx=2, sup=3, isup=4, sg=5, isg=6,
                   sg2=7, isg2=8, so=9, iso=10, gtmp=11)

        def scol(n):
            return scal[:, SCL[n]:SCL[n] + 1]

        stat = A.alloc(8, F32)
        smallf = A.alloc(256, F32)
        amax = A.alloc(16, F32)
        lnj = A.alloc(2048, BF16)
        xf32 = [A.alloc(D, F32) for _ in range(2)]
        x1t = A.alloc(D, F32)
        lntmp = [A.alloc(D, BF16) for _ in range(2)]

        t0 = []
        for src, dst in ((ones1h, ones1), (blkoh, blko), (idfh, idf),
                         (boh, bo_sb), (b_h["bq"], bsb["bq"]),
                         (b_h["bk"], bsb["bk"]), (b_h["bv"], bsb["bv"])):
            t0.append(P.op("sync", lambda e, s=src, d=dst:
                           e.dma_start(out=d, in_=s[:, :]), dma=True))
        tok_const = t0[-1]

        WAR0 = {}

        def ts(eng, out, in0, s1, s2, op0, op1=None, accum=None, deps=()):
            kw = {}
            if accum is not None:
                kw["accum_out"] = accum
            if op1 is not None:
                return P.op(eng, lambda e, o=out, i=in0, a=s1, b=s2, kw=kw:
                            e.tensor_scalar(o, i, a, b, op0, op1, **kw),
                            deps=deps, sig=True)
            return P.op(eng, lambda e, o=out, i=in0, a=s1, kw=kw:
                        e.tensor_scalar(o, i, a, None, op0, **kw),
                        deps=deps, sig=True)

        def tt(eng, out, in0, in1, op, deps=()):
            return P.op(eng, lambda e, o=out, a=in0, b=in1:
                        e.tensor_tensor(o, a, b, op), deps=deps, sig=True)

        def vcopy(out, in0, deps=()):
            return P.op("vector", lambda e, o=out, i=in0: e.tensor_copy(o, i),
                        deps=deps, sig=True)

        def scopy(out, in0, deps=()):
            return P.op("scalar", lambda e, o=out, i=in0: e.copy(o, i),
                        deps=deps, sig=True)

        def ln_stats(xtile, slot, deps):
            c = slot * 4
            su, sq = stat[:, c:c + 1], stat[:, c + 1:c + 2]
            mn, rs = stat[:, c + 2:c + 3], stat[:, c + 3:c + 4]
            w = smallf[:, 16 + 4 * slot:16 + 4 * slot + 3]
            nfd = xtile.shape[-1]
            t1 = P.op("vector", lambda e, x=xtile, o=su: e.reduce_sum(o, x, axis=AX.X),
                      deps=deps, sig=True)
            t2 = P.op("scalar", lambda e, x=xtile, o=sq, n=nfd: e.activation(
                lnj[:, 0:n], x, AF.Square, accum_out=o), deps=deps, sig=True)
            t3 = ts("vector", mn, su, 1.0 / D, None, ALU.mult, deps=[t1])
            t4 = tt("vector", w[:, 0:1], mn, mn, ALU.mult, deps=[t3])
            t5 = ts("vector", w[:, 1:2], sq, 1.0 / D, None, ALU.mult, deps=[t2])
            t6 = tt("vector", w[:, 1:2], w[:, 1:2], w[:, 0:1], ALU.subtract,
                    deps=[t4, t5])
            t7 = ts("vector", w[:, 1:2], w[:, 1:2], EPS, None, ALU.add, deps=[t6])
            t8 = P.op("scalar", lambda e, i=w[:, 1:2], o=w[:, 2:3]: e.activation(
                o, i, AF.Sqrt), deps=[t7], sig=True)
            t9 = P.op("vector", lambda e, i=w[:, 2:3], o=rs: e.reciprocal(o, i),
                      deps=[t8], sig=True)
            return mn, rs, t9, t2

        def collapse2(col_ap, deps, slot):
            ta = P.op("gpsimd", lambda e, c=col_ap: e.dma_start(
                out=col_d[0:1, :], in_=c), deps=list(deps) + [WAR0.get("colD")],
                dma=True)
            tb = P.op("sync", lambda e: e.dma_start(
                out=smallf[0:1, 64:192], in_=col_d[0:1, :]), deps=[ta], dma=True)
            WAR0["colD"] = tb
            tc = P.op("vector", lambda e: e.reduce_max(
                smallf[0:1, 40:41], smallf[0:1, 64:192], axis=AX.X),
                deps=[tb], sig=True)
            td = P.op("gpsimd", lambda e, s=slot: e.dma_start(
                out=sc_d[0:1, s:s + 1], in_=smallf[0:1, 40:41]),
                deps=[tc], dma=True)
            return td, tb

        def bcast_col(dram_ap, col, deps):
            return P.op("sync", lambda e, d=dram_ap, c=col: e.dma_start(
                out=c, in_=d.broadcast_to([128, 1])), deps=deps, dma=True)

        def absrun(src_bf, mxcol, mncol, tmpcol, deps, junk=None):
            t1 = P.op("vector", lambda e, i=src_bf, o=tmpcol: e.reduce_max(
                o, i, axis=AX.X, apply_absolute_value=True), deps=deps, sig=True)
            t2 = tt("vector", mxcol, mxcol, tmpcol, ALU.max, deps=[t1])
            return t2

        A.push()
        zbuf = [A.alloc(I3, F32) for _ in range(2)]
        ubuf0 = A.alloc(I3, F32)
        zqf = A.alloc(I3, F32)
        mxb = A.alloc(96, F32)
        ssb = A.alloc(96, F32)
        sib = A.alloc(96, F32)
        wdqb = [A.alloc(I3, BF16) for _ in range(2)]
        dq_store = []
        last_tr = None
        last_ev = {}
        prev_zq = None
        dq_list = (("up", DC, IC), ("g", DC, IC), ("d", IC, DC)) if "dq" in phases else ()
        for wname, pch, fch in dq_list:
            for c in range(pch):
                par = len(dq_store) % 2
                K, G = fch * 128, fch * 4
                t_ld = P.op("sync", lambda e, s=wZh[wname][:, c * K:(c + 1) * K],
                            d=zbuf[par][:, 0:K]: e.dma_start(out=d, in_=s),
                            deps=[dq_store[-2] if len(dq_store) >= 2 else None],
                            dma=True)
                t_mx = P.op("vector", lambda e,
                            i=zbuf[par][:, 0:K].rearrange("p (g j) -> p g j", j=32),
                            o=mxb[:, 0:G]: e.reduce_max(
                                o, i, axis=AX.X, apply_absolute_value=True),
                            deps=[t_ld], sig=True)
                t_s = ts("vector", ssb[:, 0:G], mxb[:, 0:G], 1.0 / 7.0, 1e-12,
                         ALU.mult, ALU.max, deps=[t_mx, prev_zq])
                t_si = P.op("vector", lambda e, i=ssb[:, 0:G], o=sib[:, 0:G]:
                            e.reciprocal(o, i), deps=[t_s], sig=True)
                t_u = tt("vector",
                         ubuf0[:, 0:K].rearrange("p (g j) -> p g j", j=32),
                         zbuf[par][:, 0:K].rearrange("p (g j) -> p g j", j=32),
                         sib[:, 0:G].unsqueeze(2).broadcast_to([128, G, 32]),
                         ALU.mult, deps=[t_si, last_tr, prev_zq])
                t_q = ts("vector", ubuf0[:, 0:K], ubuf0[:, 0:K], MAGIC, MAGIC,
                         ALU.add, ALU.subtract, deps=[t_u])
                t_zq = tt("vector",
                          zqf[:, 0:K].rearrange("p (g j) -> p g j", j=32),
                          ubuf0[:, 0:K].rearrange("p (g j) -> p g j", j=32),
                          ssb[:, 0:G].unsqueeze(2).broadcast_to([128, G, 32]),
                          ALU.mult, deps=[t_q, last_tr])
                prev_zq = t_zq
                ev_toks = []
                for fb0 in range(0, fch, 4):
                    nsl = min(4, fch - fb0)
                    slot = (fb0 // 4) % 4
                    bk_i = 4 + slot
                    for sl in range(nsl):
                        fb = fb0 + sl
                        dep_tr = [t_zq]
                        if sl == 0:
                            dep_tr.append(last_ev.get(slot))
                        tk_ = P.op("tensor", lambda e,
                                   o=bank(bk_i)[:, sl * 128:(sl + 1) * 128],
                                   i=zqf[:, fb * 128:(fb + 1) * 128]:
                                   e.transpose(o, i, idf),
                                   deps=dep_tr, sig=(sl == nsl - 1))
                    last_tr = tk_
                    t_ev = P.op("scalar", lambda e,
                                s=bank(bk_i)[:, 0:nsl * 128],
                                d=wdqb[par][:, fb0 * 128:(fb0 + nsl) * 128]:
                                e.copy(d, s),
                                deps=[tk_, dq_store[-2] if len(dq_store) >= 2
                                      else None], sig=True)
                    last_ev[slot] = t_ev
                    ev_toks.append(t_ev)
                t_st = P.op("gpsimd", lambda e,
                            d=wdq_h[wname][:, c * K:(c + 1) * K],
                            s=wdqb[par][:, 0:K]: e.dma_start(out=d, in_=s),
                            deps=ev_toks, dma=True)
                dq_store.append(t_st)
        tok_dq = dq_store[-1] if dq_store else tok_const
        A.pop()
        run_attn = [b for b in range(B) if f"attn{b}" in phases]

        A.push()
        qT = A.alloc(DC * S, BF16)
        kT = A.alloc(DC * S, BF16)
        vaug = A.alloc(NT * H * VROW, BF16)
        xqtok = A.alloc(D, BF16)
        xqtr = A.alloc(D, BF16)
        A.push()
        wb3 = [A.alloc(DC * D, BF16) for _ in range(3)]
        xlt = [A.alloc(DC * 128, BF16) for _ in range(2)]
        qkt = [A.alloc(D, BF16) for _ in range(2)]
        rtm = [A.alloc(D, BF16) for _ in range(2)]
        rok = [A.alloc(D, BF16) for _ in range(2)]
        ropc = [A.alloc(512, BF16) for _ in range(2)]
        rops = [A.alloc(512, BF16) for _ in range(2)]
        sqb = A.alloc(1024, BF16)
        proj_mark = A.off
        A.pop()
        A.push()
        wo_sb = A.alloc(DC * D, BF16)
        ptil = [A.alloc(4096, BF16)]
        otp = [A.alloc(8 * SB, BF16) for _ in range(2)]
        lcol = A.alloc((H // 2) * SB, BF16, rows=97)
        lrowf = A.alloc(SB, F32, rows=16)
        lrecf = A.alloc(SB, F32, rows=16)
        lrecb = A.alloc(SB, BF16, rows=16)
        rexp = A.alloc(SB, BF16)
        attnT = A.alloc(DC * SB, BF16)
        attok = A.alloc(4 * D, BF16)
        A.pop()

        vaug_r = vaug.rearrange("p (t h j) -> p t h j", h=H, j=VROW)
        qT_r = qT.rearrange("p (c t) -> p c t", t=S)
        kT_r = kT.rearrange("p (c t) -> p c t", t=S)
        wb3_r = [w.rearrange("p (c e) -> p c e", e=D) for w in wb3]
        wo_r = wo_sb.rearrange("p (c e) -> p c e", e=D)

        t_init = P.op("vector", lambda e: e.memset(vaug_r[:, :, :, 64:65], 1.0),
                      sig=True)
        t_am = P.op("vector", lambda e: e.memset(amax[:, 0:1], -3.0e38), sig=True)
        t_am2 = P.op("vector", lambda e: e.memset(amax[:, 2:3], 3.0e38), sig=True)

        tok_out_stores = []
        tok_phase = tok_dq
        war = {}

        for b in run_attn:
            tw3 = []
            for si, wk_ in enumerate(("wqT", "wkT", "wvT")):
                tw3.append(P.op("gpsimd", lambda e, w=w_h[wk_], d=wb3[si]:
                                e.dma_start(out=d, in_=w[:, :]),
                                deps=[tok_phase], dma=True))
            t_nrm0 = P.op("vector", lambda e: e.memset(amax[:, 6:8], 0.0),
                          deps=war.get("nrmD", []), sig=True)
            tq = tk = tv_ = None
            bank_guard = {}
            for tt_i in range(NT):
                par = tt_i % 2
                xt = xf32[par]
                tl = P.op("sync", lambda e, b=b, t=tt_i, d=xt: e.dma_start(
                    out=d, in_=xh[b, t * 128:(t + 1) * 128, :]),
                    deps=[tok_phase] + war.get(("xf", par), []), dma=True)
                tcv = P.op("sync", lambda e, t=tt_i, d=ropc[par]:
                           e.dma_start(out=d, in_=ropech[t * 128:(t + 1) * 128, :]),
                           deps=[war.get(("rope", par))], dma=True)
                tsv = P.op("sync", lambda e, t=tt_i, d=rops[par]:
                           e.dma_start(out=d, in_=ropesh[t * 128:(t + 1) * 128, :]),
                           deps=[war.get(("rope", par))], dma=True)
                mn, rs, tst, tsq1_ = ln_stats(xt, par, [tl])
                ta = ts("vector", lntmp[par], xt, mn, rs, ALU.subtract, ALU.mult,
                        deps=[tst] + war.get(("lntmp", par), []))
                war[("xf", par)] = [ta, tsq1_]
                ttr = P.op("sync", lambda e, src=lntmp[par], d=xlt[par]: e.dma_start(
                    out=d.rearrange("p (c q) -> p c q", q=128),
                    in_=src.rearrange("p (c q) -> p c q", q=128), transpose=True),
                    deps=[ta] + war.get(("xlt", par), []), dma=True)
                war[("lntmp", par)] = [ttr]
                xlt_r = xlt[par].rearrange("p (c q) -> p c q", q=128)
                xlt_toks = []
                for ki, kind in enumerate(("q", "k", "v")):
                    mm = []
                    for ec in range(2):
                        bk_i = 2 * ki + ec
                        dep0 = [ttr, tw3[ki], bank_guard.get(bk_i)]
                        for dc in range(DC):
                            P.op("tensor", lambda e, o=bank(bk_i),
                                 l=xlt_r[:, dc, :],
                                 r=wb3_r[ki][:, dc, ec * 512:(ec + 1) * 512],
                                 s=(dc == 0): e.matmul(o, l, r, start=s, stop=False),
                                 deps=dep0 if dc == 0 else ())
                        tm = P.op("tensor", lambda e, o=bank(bk_i),
                                  r=bsb["b" + kind[0]][:, ec * 512:(ec + 1) * 512]:
                                  e.matmul(o, ones1, r, start=False, stop=True),
                                  sig=True)
                        mm.append(tm)
                        xlt_toks.append(tm)
                    if kind == "v":
                        for ec in range(2):
                            tv = ts("vector", vaug_r[:, tt_i, ec * 8:(ec + 1) * 8, 0:64],
                                    bank(4 + ec).rearrange("p (h j) -> p h j", j=64),
                                    1.0, None, ALU.mult, deps=[mm[ec], t_init])
                            bank_guard[4 + ec] = tv
                            tv_ = tv
                    else:
                        qk = qkt[par]
                        ev = []
                        for ec in range(2):
                            bk_i = 2 * ki + ec
                            tvv = ts("vector", qk[:, ec * 512:(ec + 1) * 512],
                                     bank(bk_i), 1.0, None, ALU.mult,
                                     deps=[mm[ec]] + war.get(("qkt", par), []))
                            bank_guard[bk_i] = tvv
                            ev.append(tvv)
                        qk_r = qk.rearrange("p (h j) -> p h j", j=64)
                        x1v, x2v = qk_r[:, :, 0:32], qk_r[:, :, 32:64]
                        rkey = "lntmp" if kind == "q" else "rok"
                        ro = lntmp[par] if kind == "q" else rok[par]
                        ro_r = ro.rearrange("p (h j) -> p h j", j=64)
                        o1, o2 = ro_r[:, :, 0:32], ro_r[:, :, 32:64]
                        rt_r = rtm[par].rearrange("p (h j) -> p h j", j=64)
                        u1, u2 = rt_r[:, :, 0:32], rt_r[:, :, 32:64]
                        cv = ropc[par].rearrange("p (h j) -> p h j", j=32)
                        sv = rops[par].rearrange("p (h j) -> p h j", j=32)
                        dr = ev + [tcv, tsv] + war.get(("rtm", par), [])
                        r1 = tt("vector", u1, x1v, cv, ALU.mult, deps=dr)
                        r2 = tt("vector", u2, x2v, sv, ALU.mult, deps=dr)
                        r3 = tt("vector", o1, u1, u2, ALU.subtract,
                                deps=[r1, r2] + war.get((rkey, par), []))
                        r4 = tt("vector", u1, x1v, sv, ALU.mult, deps=[r3])
                        r5 = tt("vector", u2, x2v, cv, ALU.mult, deps=[r3])
                        r6 = tt("vector", o2, u1, u2, ALU.add, deps=[r4, r5])
                        dstT = qT_r if kind == "q" else kT_r
                        ttr2 = P.op("sync", lambda e, src=ro, t=tt_i, d=dstT:
                                    e.dma_start(
                                        out=d[:, :, t * 128:(t + 1) * 128],
                                        in_=src.rearrange("p (c q) -> p c q", q=128),
                                        transpose=True), deps=[r6], dma=True)
                        nacc = amax[:, 8 + 2 * ki + par:9 + 2 * ki + par]
                        tsqn = P.op("scalar", lambda e, i=ro, c=nacc:
                                    e.activation(lnj[:, 0:D], i, AF.Square,
                                                 accum_out=c),
                                    deps=[r6] + war.get(("nacc", ki, par), []),
                                    sig=True)
                        nmg = tt("vector", amax[:, 6 + ki:7 + ki],
                                 amax[:, 6 + ki:7 + ki], nacc, ALU.max,
                                 deps=[tsqn, t_nrm0] + war.get(("nrun", ki), []))
                        war[("nacc", ki, par)] = [nmg]
                        war[("nrun", ki)] = [nmg]
                        war[(rkey, par)] = [ttr2, tsqn]
                        war[("rope", par)] = r6
                        war[("rtm", par)] = [r6]
                        war[("qkt", par)] = [r4, r5]
                        if kind == "q":
                            tq = ttr2
                        else:
                            tk = ttr2
                war[("xlt", par)] = xlt_toks
            for p_ in range(3):
                war[("bankpair", p_)] = [bank_guard.get(2 * p_),
                                         bank_guard.get(2 * p_ + 1)]
            tok_wo = P.op("gpsimd", lambda e: e.dma_start(out=wo_sb, in_=w_h["woT"][:, :]),
                          deps=[tv_], dma=True)

            tnq = P.op("gpsimd", lambda e: e.dma_start(
                out=ncol_d[0:1, :], in_=amax[:, 6:7]),
                deps=war.get(("nrun", 0), []) + [WAR0.get("colD")], dma=True)
            tnk = P.op("gpsimd", lambda e: e.dma_start(
                out=ncol_d[1:2, :], in_=amax[:, 7:8]),
                deps=war.get(("nrun", 1), []), dma=True)
            war["nrmD"] = [tnq, tnk]
            tn4 = P.op("sync", lambda e: e.dma_start(
                out=smallf[0:2, 64:192], in_=ncol_d[:, :]),
                deps=[tnq, tnk, WAR0.get("colD")], dma=True)
            tn5 = P.op("vector", lambda e: e.reduce_max(
                smallf[0:2, 42:43], smallf[0:2, 64:192], axis=AX.X),
                deps=[tn4], sig=True)
            tn7 = P.op("scalar", lambda e: e.activation(
                smallf[0:2, 43:44], smallf[0:2, 42:43], AF.Sqrt),
                deps=[tn5], sig=True)
            tn3 = P.op("gpsimd", lambda e: e.dma_start(
                out=col_d[0:1, 0:2], in_=smallf[0:2, 43:44]),
                deps=[tn7, WAR0.get("colD")], dma=True)
            tn4b = P.op("sync", lambda e: e.dma_start(
                out=smallf[0:1, 48:50], in_=col_d[0:1, 0:2]),
                deps=[tn3], dma=True)
            WAR0["colD"] = tn4b
            tn9 = tt("vector", smallf[0:1, 56:57], smallf[0:1, 48:49],
                     smallf[0:1, 49:50], ALU.mult, deps=[tn4b])
            tn10 = ts("vector", smallf[0:1, 57:58], smallf[0:1, 56:57], -0.125,
                      None, ALU.mult, deps=[tn9])
            tn11 = P.op("gpsimd", lambda e: e.dma_start(
                out=sc_d[0:1, 16:17], in_=smallf[0:1, 57:58]),
                deps=[tn10, war.get("scD16")], dma=True)
            tok_m0 = bcast_col(sc_d[0:1, 16:17], scol("negm0"),
                               [tn11, war.get("negm0")])
            war["scD16"] = tok_m0

            def emit_pipe(sb_i, mid=None):
                buf = sb_i % 2
                evs = {}
                for h in range(H):
                    if mid and h in (2, 5, 8, 11):
                        mid()
                    hp, odd = h // 2, h % 2
                    r0 = 64 * odd
                    tpos = (64, 0) if odd else (0, 0)
                    pairbanks = ((0, 1), (2, 3), (4, 5))
                    expt = {}
                    mmt = {}
                    pv = None
                    for pr in range(NT // 2 + 1):
                        if pr < NT // 2:
                            bks = pairbanks[pr % 3]
                            for j in range(2):
                                tc = 2 * pr + j
                                depq = [tv_, expt.get(pr - 3)] + (
                                    war.get(("bankpair", pr % 3), [])
                                    if pr < 3 else [])
                                mmt[pr] = P.op("tensor", lambda e,
                                    o=bank(bks[j], rows=128),
                                    l=kT_r[r0:r0 + 64, hp, tc * 128:(tc + 1) * 128],
                                    r=qT_r[r0:r0 + 64, hp, sb_i * SB:(sb_i + 1) * SB],
                                    tp=tpos: e.matmul(o, l, r, start=True,
                                                      stop=True, tile_position=tp),
                                    deps=depq if j == 0 else (), sig=(j == 1))
                            expt[pr] = P.op("scalar", lambda e,
                                o=ptil[0][:, (pr % 4) * 1024:(pr % 4 + 1) * 1024],
                                i=bank(bks[0], n=2),
                                : e.activation(o, i, AF.Exp, bias=scol("negm0"),
                                               scale=0.125),
                                deps=[mmt[pr], tok_m0, war.get(("ptil", pr % 4))],
                                sig=True)
                            war[("bankpair", pr % 3)] = [expt[pr]]
                        if pr >= 1:
                            po = pr - 1
                            for j in range(2):
                                tc = 2 * po + j
                                depv = [expt[po]] + (war.get("pv_evac", [])
                                                     if tc == 0 else [])
                                pv = P.op("tensor", lambda e, o=bank(6, rows=65),
                                    l=vaug_r[:, tc, h, 0:65],
                                    r=ptil[0][:, (po % 4) * 1024 + j * 512:
                                              (po % 4) * 1024 + (j + 1) * 512],
                                    s=(tc == 0), p=(tc == NT - 1):
                                    e.matmul(o, l, r, start=s, stop=p),
                                    deps=depv if (j == 0 or tc == 0) else (),
                                    sig=(tc == NT - 1))
                            war[("ptil", po % 4)] = pv
                    tev = vcopy(otp[buf][r0:r0 + 64, hp * SB:(hp + 1) * SB],
                                bank(6, rows=64),
                                deps=[pv, war.get(("otp", buf, hp))])
                    lr = r0 + 32 * buf
                    tl = vcopy(lcol[lr:lr + 1, hp * SB:(hp + 1) * SB],
                               psap[64:65, 6 * 512:7 * 512],
                               deps=[pv] + war.get(("lcol", buf), []))
                    war["pv_evac"] = [tev, tl]
                    war["last_exp"] = expt[NT // 2 - 1]
                    evs[h] = (tev, tl)
                return [(hp, evs[2 * hp][0], evs[2 * hp + 1][0],
                         evs[2 * hp][1], evs[2 * hp + 1][1])
                        for hp in range(H // 2)]

            def emit_lchain(sb_i, pair_toks):
                buf = sb_i % 2
                tg1 = P.op("gpsimd", lambda e: e.dma_start(
                    out=lcol_d[0:1, :], in_=lcol[32 * buf:32 * buf + 1, :]),
                    deps=[t[3] for t in pair_toks] + [war.get("lrowf")], dma=True)
                tg2 = P.op("gpsimd", lambda e: e.dma_start(
                    out=lcol_d[1:2, :], in_=lcol[64 + 32 * buf:65 + 32 * buf, :]),
                    deps=[t[4] for t in pair_toks], dma=True)
                tg3 = P.op("gpsimd", lambda e: e.dma_start(
                    out=lrowf, in_=lcol_d[:, :].rearrange("r (h j) -> (r h) j", j=SB)),
                    deps=[tg1, tg2], dma=True)
                war[("lcol", buf)] = [tg1, tg2]
                tg4 = P.op("vector", lambda e: e.reciprocal(lrecf, lrowf),
                           deps=[tg3], sig=True)
                war["lrowf"] = tg4
                tg5 = ts("vector", lrecb, lrecf, 1.0, None, ALU.mult, deps=[tg4])
                tg6 = P.op("gpsimd", lambda e: e.dma_start(
                    out=lrec_d[:, :], in_=lrecb),
                    deps=[tg5, war.get("lrecd")], dma=True)
                pre0a = P.op("sync", lambda e: e.dma_start(
                    out=rexp[0:64, :],
                    in_=lrec_d[0:1, :].broadcast_to([64, SB])),
                    deps=[tg6, war.get("rexp")], dma=True)
                pre0b = P.op("sync", lambda e: e.dma_start(
                    out=rexp[64:128, :],
                    in_=lrec_d[8:9, :].broadcast_to([64, SB])),
                    deps=[tg6, war.get("rexp")], dma=True)
                return tg6, pre0a, pre0b

            def emit_tail(sb_i, pair_toks, lch):
                buf = sb_i % 2
                tg6, pre0a, pre0b = lch
                div_toks = []
                rexp_guard = None
                for hp, tev1, tev2, _, _ in pair_toks:
                    if hp == 0:
                        tr1a, tr1 = pre0a, pre0b
                    else:
                        tr1a = P.op("sync", lambda e, hp=hp: e.dma_start(
                            out=rexp[0:64, :],
                            in_=lrec_d[hp:hp + 1, :].broadcast_to([64, SB])),
                            deps=[tg6, rexp_guard], dma=True)
                        tr1 = P.op("sync", lambda e, hp=hp: e.dma_start(
                            out=rexp[64:128, :],
                            in_=lrec_d[8 + hp:9 + hp, :].broadcast_to([64, SB])),
                            deps=[tg6, rexp_guard], dma=True)
                    td = tt("gpsimd", otp[buf][:, hp * SB:(hp + 1) * SB],
                            otp[buf][:, hp * SB:(hp + 1) * SB], rexp, ALU.mult,
                            deps=[tr1a, tr1, tev1, tev2])
                    rexp_guard = td
                    div_toks.append(td)
                war["rexp"] = rexp_guard
                war["lrecd"] = div_toks[-1]
                ev2 = []
                bank6_guard = war.get("bank7")
                for fc in range(DC):
                    for pp in range(DC):
                        tmo = P.op("tensor", lambda e, o=bank(7),
                                   l=wo_r[:, pp, fc * 128:(fc + 1) * 128],
                                   r=otp[buf][:, pp * SB:(pp + 1) * SB],
                                   s=(pp == 0), p=(pp == DC - 1):
                                   e.matmul(o, l, r, start=s, stop=p),
                                   deps=[div_toks[-1], tok_wo, bank6_guard]
                                   if pp == 0 else (), sig=(pp == DC - 1))
                    tev = ts("vector", attnT[:, fc * SB:(fc + 1) * SB], bank(7),
                             bo_sb[:, fc:fc + 1], None, ALU.add,
                             deps=[tmo, war.get("attnT")])
                    bank6_guard = tev
                    ev2.append(tev)
                war["bank7"] = bank6_guard
                for hp in range(DC):
                    war[("otp", buf, hp)] = ev2[-1]
                ttr3 = None
                for fc in range(DC):
                    ttr3 = P.op("sync", lambda e, fc=fc: e.dma_start(
                        out=attok.rearrange("p (w f q) -> p w f q", f=DC, q=128
                                            )[:, :, fc, :],
                        in_=attnT[:, fc * SB:(fc + 1) * SB],
                        transpose=True),
                        deps=[ev2[fc]] + war.get("attok", []), dma=True)
                war["attnT"] = ttr3
                return ttr3

            def emit_tail_ln(sb_i, ttr3):
                for tw in range(4):
                    gt = sb_i * 4 + tw
                    par = gt % 2
                    atile = attok[:, tw * D:(tw + 1) * D]
                    mn, rs, tst, tsq_ = ln_stats(atile, par, [ttr3])
                    t_ln2 = ts("vector", lntmp[par], atile, mn, rs,
                               ALU.subtract, ALU.mult,
                               deps=[tst] + war.get(("lntmp", par), []))
                    t_xl = P.op("sync", lambda e, d=xf32[par], b=b, g=gt:
                                e.dma_start(out=d, in_=xh[b, g * 128:(g + 1) * 128, :]),
                                deps=war.get(("xf", par), []), dma=True)
                    t_x1 = tt("vector", x1t, xf32[par], lntmp[par], ALU.add,
                              deps=[t_xl, t_ln2] + war.get("x1t", []))
                    war[("xf", par)] = [t_x1]
                    t_x1s = P.op("gpsimd", lambda e, b=b, g=gt: e.dma_start(
                        out=x1_scr[b * NT + g], in_=x1t), deps=[t_x1], dma=True)
                    mn2, rs2, tst2, tsq2_ = ln_stats(x1t, par, [t_x1])
                    t_xq = ts("vector", xqtok, x1t, mn2, rs2, ALU.subtract,
                              ALU.mult, deps=[tst2, war.get("xqtok")])
                    war["x1t"] = [t_x1s, tsq2_]
                    t_ab = absrun(xqtok, amax[:, 0:1], amax[:, 2:3],
                                  amax[:, 4:5], [t_xq, t_am, t_am2])
                    t_tr4 = P.op("sync", lambda e: e.dma_start(
                        out=xqtr.rearrange("p (c q) -> p c q", q=128),
                        in_=xqtok.rearrange("p (c q) -> p c q", q=128),
                        transpose=True), deps=[t_xq, war.get("xqtr")], dma=True)
                    war["xqtok"] = t_tr4
                    t_xqs = P.op("gpsimd", lambda e, b=b, g=gt: e.dma_start(
                        out=xq_scr.rearrange("p (c t) -> p c t", t=B * S)[
                            :, :, (b * S + g * 128):(b * S + (g + 1) * 128)],
                        in_=xqtr.rearrange("p (c q) -> p c q", q=128)),
                        deps=[t_tr4], dma=True)
                    war["xqtr"] = t_xqs
                    war[("lntmp", par)] = [t_x1]
                    war["attok"] = [tsq_, t_ln2]
                    lnres["tok"] = t_xqs
                    yield

            pend = []
            lnq = []
            lnres = {"tok": None}

            def _lnstep(n):
                while lnq and n > 0:
                    try:
                        next(lnq[0])
                        n -= 1
                    except StopIteration:
                        lnq.pop(0)

            def _mid():
                _lnstep(1)

            for sb_i in range(NSB):
                pt = emit_pipe(sb_i, _mid)
                if pend:
                    sbp, ptp, lchp = pend.pop(0)
                    t3_ = emit_tail(sbp, ptp, lchp)
                    lnq.append(emit_tail_ln(sbp, t3_))
                pend.append((sb_i, pt, emit_lchain(sb_i, pt)))
            sbp, ptp, lchp = pend.pop(0)
            _lnstep(99)
            t3_ = emit_tail(sbp, ptp, lchp)
            for _ in emit_tail_ln(sbp, t3_):
                pass
            tok_phase = lnres["tok"]
            war["negm0"] = war.get("last_exp")
        tok_attn_done = tok_phase
        A.pop()

        A.push()
        wff = A.alloc(DC * I3, BF16)
        mxc = {k: A.alloc(4, F32) for k in ("up", "gate", "g2", "o")}
        acc_t = A.alloc(2, F32)
        A.push()
        wff2 = A.alloc(DC * I3, BF16)
        xqb = [A.alloc(DC * 512, BF16) for _ in range(2)]
        xqq = [A.alloc(DC * 512, BF16) for _ in range(2)]
        qtmp = A.alloc(DC * 512, F32)
        spb = [A.alloc(2048, BF16) for _ in range(2)]
        A.pop()
        A.push()
        gb = [A.alloc(4096, BF16) for _ in range(2)]
        g2b = [A.alloc(4096, BF16) for _ in range(2)]
        ub = [A.alloc(4096, BF16) for _ in range(2)]
        sgb = [A.alloc(4096, BF16) for _ in range(2)]
        ftmp = [A.alloc(4096, F32) for _ in range(2)]
        A.pop()
        A.push()
        ob = A.alloc(IC * 512, BF16)
        oq = A.alloc(IC * 512, BF16)
        dnT = A.alloc(DC * 512, BF16)
        dntok = A.alloc(4 * D, BF16)
        ftmp4 = [A.alloc(2048, F32) for _ in range(2)]
        A.pop()

        def ar(idx, cols, names, deps):
            toks = []
            guard = list(deps) + [WAR0.get("ar")]
            for i, (mxcol, mncol) in enumerate(cols):
                tcx, _ = collapse2(mxcol, guard, i)
                toks.append(P.op("gpsimd", lambda e, i=i, idx=idx: e.dma_start(
                    out=cc_io[idx][0][0:1, i:i + 1], in_=sc_d[0:1, i:i + 1]),
                    deps=[tcx], dma=True))
                guard = [tcx]
            tcc_ = P.op("gpsimd", lambda e, idx=idx: e.collective_compute(
                "AllReduce", ALU.max, replica_groups=[list(range(NCORES))],
                ins=[cc_io[idx][0].opt()], outs=[cc_io[idx][1].opt()]),
                deps=toks, cc=True)
            last = None
            for i, (sn, isn) in enumerate(names):
                tbx = bcast_col(cc_io[idx][1][0:1, i:i + 1], scol("gtmp"),
                                [tcc_, last])
                tsa = ts("vector", scol(sn), scol("gtmp"), 1.0 / QS, 1e-12,
                         ALU.mult, ALU.max, deps=[tbx])
                last = P.op("vector", lambda e, sn=sn, isn=isn: e.reciprocal(
                    scol(isn), scol(sn)), deps=[tsa], sig=True)
            WAR0["ar"] = last
            return last

        def _ffn_body():
            tok_sx = ar(0, [(amax[:, 0:1], amax[:, 2:3])], [("sx", "isx")],
                        [tok_attn_done])

            t_wu = P.op("sync", lambda e: e.dma_start(out=wff, in_=wdq_h["up"][:, :]),
                        deps=[tok_dq, tok_attn_done], dma=True)
            t_wg = P.op("sync", lambda e: e.dma_start(out=wff2, in_=wdq_h["g"][:, :]),
                        deps=[tok_dq, tok_attn_done], dma=True)
            for k in ("up", "gate", "g2", "o"):
                P.op("vector", lambda e, k=k: e.memset(mxc[k][:, 0:1], -3.0e38), sig=True)
                P.op("vector", lambda e, k=k: e.memset(mxc[k][:, 1:2], 3.0e38), sig=True)
            wff_r = wff.rearrange("p (c i) -> p c i", i=I3)
            wff2_r = wff2.rearrange("p (c i) -> p c i", i=I3)
            xq_scr_r = xq_scr.rearrange("p (c t) -> p c t", t=B * S)

            last_sp = {}
            bank_g = {}
            last_tg = [None, None]
            for tg_i in range(TG):
                par = tg_i % 2
                tld = P.op("sync", lambda e, t=tg_i, d=xqb[par]: e.dma_start(
                    out=d.rearrange("p (c q) -> p c q", q=512),
                    in_=xq_scr_r[:, :, t * 512:(t + 1) * 512]),
                    deps=[tok_sx, last_tg[par]], dma=True)
                tq1 = ts("vector", qtmp, xqb[par], scol("isx"), MAGIC, ALU.mult,
                         ALU.add, deps=[tld, tok_sx, war.get("qtmp")])
                tq2 = ts("vector", xqq[par], qtmp, MAGIC, scol("sx"), ALU.subtract,
                         ALU.mult, deps=[tq1, tok_sx])
                war["qtmp"] = tq2
                last_tg[par] = tq2
                xqr = xqq[par].rearrange("p (c q) -> p c q", q=512)
                for wi, (wr, scr, mx) in enumerate(
                        ((wff_r, spill["up"], mxc["up"]),
                         (wff2_r, spill["gate"], mxc["gate"]))):
                    for icg in range(IC // 4):
                        evg = []
                        sp_par = (tg_i * 2 + icg) % 2
                        for s4 in range(4):
                            ic = icg * 4 + s4
                            bk_i = s4 % 2 + 2 * wi
                            for dc in range(DC):
                                tmf = P.op("tensor", lambda e, o=bank(bk_i),
                                           l=wr[:, dc, ic * 128:(ic + 1) * 128],
                                           r=xqr[:, dc, :], s=(dc == 0),
                                           p=(dc == DC - 1):
                                           e.matmul(o, l, r, start=s, stop=p),
                                           deps=[tq2, t_wu, t_wg, bank_g.get(bk_i)]
                                           if dc == 0 else (), sig=(dc == DC - 1))
                            tev = P.op("scalar", lambda e,
                                       o=spb[sp_par][:, s4 * 512:(s4 + 1) * 512],
                                       i=bank(bk_i): e.copy(o, i),
                                       deps=[tmf, last_sp.get(sp_par)], sig=True)
                            bank_g[bk_i] = tev
                            evg.append(tev)
                            ta = absrun(spb[sp_par][:, s4 * 512:(s4 + 1) * 512],
                                        mx[:, 0:1], mx[:, 1:2], acc_t[:, 0:1], [tev])
                            evg.append(ta)
                        tst_ = P.op("gpsimd", lambda e, t=tg_i, icg=icg, scr=scr,
                                    sp=sp_par: e.dma_start(
                                        out=scr[t, icg * 4:(icg + 1) * 4].transpose([1, 0, 2]),
                                        in_=spb[sp].rearrange("p (a q) -> p a q", q=512)),
                                    deps=evg, dma=True)
                        last_sp[sp_par] = tst_
            tok_ar2 = ar(1, [(mxc["up"][:, 0:1], mxc["up"][:, 1:2]),
                             (mxc["gate"][:, 0:1], mxc["gate"][:, 1:2])],
                         [("sup", "isup"), ("sg", "isg")],
                         [last_sp[0], last_sp[1]])

            prevs = {}
            p2_last = None
            for gidx in range(TG * (IC // 8)):
                tg_i, icg = divmod(gidx, IC // 8)
                par = gidx % 2
                tld = P.op("sync", lambda e, t=tg_i, icg=icg, d=gb[par]: e.dma_start(
                    out=d.rearrange("p (a q) -> p a q", q=512),
                    in_=spill["gate"][t, icg * 8:(icg + 1) * 8].transpose([1, 0, 2])),
                    deps=[tok_ar2, prevs.get(("st", par))], dma=True)
                m1 = ts("vector", ftmp[par], gb[par], scol("isg"), MAGIC, ALU.mult,
                        ALU.add, deps=[tld, prevs.get(("ft", par))])
                m2 = ts("vector", gb[par], ftmp[par], MAGIC, scol("sg"), ALU.subtract,
                        ALU.mult, deps=[m1, tok_ar2])
                m3 = P.op("scalar", lambda e, i=gb[par], o=sgb[par]: e.activation(
                    o, i, AF.Sigmoid), deps=[m2, prevs.get(("sg", par))], sig=True)
                m4 = ts("vector", ftmp[par], sgb[par], 65536.0, MAGIC, ALU.mult,
                        ALU.add, deps=[m3, m2])
                m5 = ts("vector", sgb[par], ftmp[par], MAGIC, 65535.0, ALU.subtract,
                        ALU.min, deps=[m4])
                prevs[("ft", par)] = m5
                m6 = tt("vector", g2b[par], gb[par], sgb[par], ALU.mult,
                        deps=[m5, prevs.get(("g2", par))])
                prevs[("sg", par)] = m6
                m7 = absrun(g2b[par], mxc["g2"][:, 0:1], mxc["g2"][:, 1:2],
                            acc_t[:, 0:1], [m6], junk=gb[par])
                tst_ = P.op("gpsimd", lambda e, t=tg_i, icg=icg, par=par: e.dma_start(
                    out=spill["g2"][t, icg * 8:(icg + 1) * 8].transpose([1, 0, 2]),
                    in_=g2b[par].rearrange("p (a q) -> p a q", q=512)),
                    deps=[m7, m6], dma=True)
                prevs[("g2", par)] = tst_
                prevs[("st", par)] = tst_
                p2_last = tst_
            tok_ar3 = ar(2, [(mxc["g2"][:, 0:1], mxc["g2"][:, 1:2])],
                         [("sg2", "isg2")],
                         [prevs.get(("st", 0)), prevs.get(("st", 1))])

            prevs = {}
            for gidx in range(TG * (IC // 8)):
                tg_i, icg = divmod(gidx, IC // 8)
                par = gidx % 2
                tld1 = P.op("sync", lambda e, t=tg_i, icg=icg, d=gb[par]: e.dma_start(
                    out=d.rearrange("p (a q) -> p a q", q=512),
                    in_=spill["g2"][t, icg * 8:(icg + 1) * 8].transpose([1, 0, 2])),
                    deps=[tok_ar3, prevs.get(("st", par))], dma=True)
                tld2 = P.op("sync", lambda e, t=tg_i, icg=icg, d=ub[par]: e.dma_start(
                    out=d.rearrange("p (a q) -> p a q", q=512),
                    in_=spill["up"][t, icg * 8:(icg + 1) * 8].transpose([1, 0, 2])),
                    deps=[tok_ar3, prevs.get(("st", par))], dma=True)
                m1 = ts("vector", ftmp[par], gb[par], scol("isg2"), MAGIC, ALU.mult,
                        ALU.add, deps=[tld1, prevs.get(("ft", par))])
                m2 = ts("vector", gb[par], ftmp[par], MAGIC, scol("sg2"), ALU.subtract,
                        ALU.mult, deps=[m1, tok_ar3])
                m3 = ts("vector", ftmp[par], ub[par], scol("isup"), MAGIC, ALU.mult,
                        ALU.add, deps=[tld2, m2])
                m4 = ts("vector", ub[par], ftmp[par], MAGIC, scol("sup"), ALU.subtract,
                        ALU.mult, deps=[m3, tok_ar2])
                prevs[("ft", par)] = m4
                m5 = tt("vector", g2b[par], gb[par], ub[par], ALU.mult,
                        deps=[m4, m2, prevs.get(("g2", par))])
                m6 = absrun(g2b[par], mxc["o"][:, 0:1], mxc["o"][:, 1:2],
                            acc_t[:, 0:1], [m5], junk=gb[par])
                tst_ = P.op("gpsimd", lambda e, t=tg_i, icg=icg, par=par: e.dma_start(
                    out=spill["o"][t, icg * 8:(icg + 1) * 8].transpose([1, 0, 2]),
                    in_=g2b[par].rearrange("p (a q) -> p a q", q=512)),
                    deps=[m6, m5], dma=True)
                prevs[("g2", par)] = tst_
                prevs[("st", par)] = tst_
            tok_ar4 = ar(3, [(mxc["o"][:, 0:1], mxc["o"][:, 1:2])], [("so", "iso")],
                         [prevs.get(("st", 0)), prevs.get(("st", 1))])

            t_wd = P.op("sync", lambda e: e.dma_start(out=wff, in_=wdq_h["d"][:, :]),
                        deps=[tok_ar2], dma=True)
            wd_r = wff.rearrange("p (c i) -> p c i", i=D)
            war2 = {}
            for tg_i in range(TG):
                tld = P.op("sync", lambda e, t=tg_i: e.dma_start(
                    out=ob.rearrange("p (a q) -> p a q", q=512),
                    in_=spill["o"][t].transpose([1, 0, 2])),
                    deps=[tok_ar4, war2.get("ob")], dma=True)
                q_toks = []
                qprev = war2.get("oq")
                fprev = None
                for icg in range(IC // 4):
                    fpar = icg % 2
                    m1 = ts("vector", ftmp4[fpar], ob[:, icg * 2048:(icg + 1) * 2048],
                            scol("iso"), MAGIC, ALU.mult, ALU.add,
                            deps=[tld, fprev])
                    m2 = ts("vector", oq[:, icg * 2048:(icg + 1) * 2048], ftmp4[fpar],
                            MAGIC, scol("so"), ALU.subtract, ALU.mult,
                            deps=[m1, qprev, tok_ar4])
                    fprev = m2
                    q_toks.append(m2)
                war2["ob"] = q_toks[-1]
                oqr = oq.rearrange("p (c q) -> p c q", q=512)
                ev3 = []
                bank_pg = {}
                for h2c in range(DC):
                    bk_i = h2c % 4
                    for qc in range(IC):
                        tmf = P.op("tensor", lambda e, o=bank(bk_i),
                                   l=wd_r[:, qc, h2c * 128:(h2c + 1) * 128],
                                   r=oqr[:, qc, :], s=(qc == 0), p=(qc == IC - 1):
                                   e.matmul(o, l, r, start=s, stop=p),
                                   deps=[q_toks[qc // 4], t_wd, bank_pg.get(bk_i),
                                         war2.get(("bank", bk_i))]
                                   if qc % 4 == 0 else (), sig=(qc == IC - 1))
                    tev = ts("vector", dnT[:, h2c * 512:(h2c + 1) * 512], bank(bk_i),
                             1.0 / 65536.0, None, ALU.mult,
                             deps=[tmf, war2.get("dnT")])
                    bank_pg[bk_i] = tev
                    war2[("bank", bk_i)] = tev
                    ev3.append(tev)
                war2["oq"] = ev3[-1]
                ttr5 = None
                for fc in range(DC):
                    ttr5 = P.op("sync", lambda e, fc=fc: e.dma_start(
                        out=dntok.rearrange("p (w f q) -> p w f q", f=DC, q=128
                                            )[:, :, fc, :],
                        in_=dnT[:, fc * SB:(fc + 1) * SB],
                        transpose=True), deps=[ev3[fc]] + [war2.get("dntok")], dma=True)
                war2["dnT"] = ttr5
                last_sq = None
                for tw in range(4):
                    gt = tg_i * 4 + tw
                    b_i, tt_i = gt // NT, gt % NT
                    par = gt % 2
                    atile = dntok[:, tw * D:(tw + 1) * D]
                    mn, rs, tst_, tsq_ = ln_stats(atile, par, [ttr5])
                    t_ln4 = ts("vector", lntmp[par], atile, mn, rs, ALU.subtract,
                               ALU.mult, deps=[tst_] + war.get(("lntmp", par), []))
                    t_xl = P.op("sync", lambda e, d=xf32[par], g=gt: e.dma_start(
                        out=d, in_=x1_scr[g]), deps=war.get(("xf", par), []), dma=True)
                    t_o = tt("vector", x1t, xf32[par], lntmp[par], ALU.add,
                             deps=[t_xl, t_ln4] + war.get("x1t", []))
                    war[("xf", par)] = [t_o]
                    t_os = P.op("gpsimd", lambda e, b=b_i, t=tt_i: e.dma_start(
                        out=outh[b, t * 128:(t + 1) * 128, :], in_=x1t),
                        deps=[t_o], dma=True)
                    war["x1t"] = [t_os]
                    war[("lntmp", par)] = [t_o]
                    last_sq = tsq_
                    tok_out_stores.append(t_os)
                war2["dntok"] = last_sq
            A.pop()

        if "ffn" in phases:
            _ffn_body()

        if not tok_out_stores:
            tok_out_stores.append(tok_attn_done)
        if tap == "ffn4":
            t_dbg = None
            for i, k in enumerate(("up", "gate", "g2", "o")):
                t_dbg = P.op("gpsimd", lambda e, s=spill[k], i=i: e.dma_start(
                    out=dbg[i], in_=s), deps=[tok_out_stores[-1]], dma=True)
            tok_out_stores.append(t_dbg)
        elif tap is not None:
            srcs = {"x1": x1_scr, "xq": xq_scr, "up": spill["up"],
                    "gate": spill["gate"], "g2": spill["g2"], "o": spill["o"],
                    "wupdq": wdq_h["up"], "wgdq": wdq_h["g"], "wddq": wdq_h["d"]}
            t_dbg = P.op("gpsimd", lambda e, s=srcs[tap]: e.dma_start(
                out=dbg, in_=s), deps=[tok_out_stores[-1]], dma=True)
            tok_out_stores.append(t_dbg)

        P.op("gpsimd", lambda e: e.engine_nop(), deps=[tok_out_stores[-1]])

        block = es.enter_context(nc.Block())

        @block.sync
        def _(e):
            P.replay("sync", e, sems)

        @block.gpsimd
        def _(e):
            P.replay("gpsimd", e, sems)

        @block.tensor
        def _(e):
            P.replay("tensor", e, sems)

        @block.vector
        def _(e):
            P.replay("vector", e, sems)

        @block.scalar
        def _(e):
            P.replay("scalar", e, sems)

    build_program.last_plan = P
    return nc


def kernel(**inputs):
    in_maps = _stage(inputs)
    nc = build_program()
    res = run_bass_kernel_spmd(nc, in_maps, core_ids=list(range(NCORES)))
    return np.concatenate(
        [np.asarray(r["out"]) for r in res.results], axis=0).astype(np.float32)
